# version 03
# baseline (speedup 1.0000x reference)
import sys
sys.path.insert(0, "/opt/trn_rl_repo")
import numpy as np
import concourse.bass as bass
import concourse.mybir as mybir
from contextlib import ExitStack

F32 = mybir.dt.float32
BF16 = mybir.dt.bfloat16
I32 = mybir.dt.int32
U8 = mybir.dt.uint8
AF = mybir.ActivationFunctionType
ALU = mybir.AluOpType
AX = mybir.AxisListType
ds = bass.ds

NCORES = 8
D = 2048
T = 2048
TG = 16384
HEADS = 16
DH = 128
HPC = 2
SEQ = 4096
NB = 4
KC = D // 128
EPS_LN = 1e-5
EPS_RMS = 1e-6
CHUNK = 128
NCH = SEQ // CHUNK


def lrpe_tables():
    half = DH // 2
    theta = (10000.0 ** (-2.0 * np.arange(half, dtype=np.float64) / DH)).astype(np.float32)
    pos = np.arange(SEQ, dtype=np.float32)
    ang = (pos[None, :] * theta[:, None]).astype(np.float32)
    cos = np.cos(ang.astype(np.float64)).astype(np.float32)
    sin = np.sin(ang.astype(np.float64)).astype(np.float32)
    return cos, sin


def alibi_log_lam():
    slopes = 2.0 ** (-8.0 * (np.arange(HEADS, dtype=np.float64) + 1.0) / HEADS)
    return -slopes


def lightning_consts(core):
    ll = alibi_log_lam()
    i = np.arange(CHUNK, dtype=np.float64)
    dt = np.zeros((HPC, CHUNK, CHUNK), np.float32)
    dq = np.zeros((HPC, CHUNK, CHUNK), np.float32)
    dk = np.zeros((HPC, CHUNK), np.float32)
    lamc = np.zeros((HPC,), np.float32)
    for s in range(HPC):
        h = 2 * core + s
        diff = i[None, :] - i[:, None]
        dt[s] = np.where(diff >= 0, np.exp(ll[h] * diff), 0.0).astype(np.float32)
        dq[s] = (np.exp(ll[h] * (i[None, :] + 1.0)) * np.ones((CHUNK, 1))).astype(np.float32)
        dk[s] = np.exp(ll[h] * (CHUNK - 1.0 - i)).astype(np.float32)
        lamc[s] = np.float32(np.exp(ll[h] * CHUNK))
    return dt, dq, dk, lamc


class S:
    def __init__(self, nc, name):
        self.h = nc.alloc_semaphore(name=name)
        self.n = 0

    def inc(self, instr, k=1):
        instr.then_inc(self.h, k)
        self.n += k
        return (self, self.n)

    def wait(self, eng, n=None):
        eng.wait_ge(self.h, self.n if n is None else n)


def wait_ev(eng, ev):
    if ev is not None:
        sem, n = ev
        eng.wait_ge(sem.h, n)


class Ctx:
    def __init__(self, nc):
        self.nc = nc
        self.pe = S(nc, "s_pe")
        self.act = S(nc, "s_act")
        self.dve = S(nc, "s_dve")
        self.pool = S(nc, "s_pool")
        self._sems = {}

    def sem(self, name):
        if name not in self._sems:
            self._sems[name] = S(self.nc, name)
        return self._sems[name]

    def dma(self, eng, out, in_, sem):
        return sem.inc(eng.dma_start(out=out, in_=in_), 16)

    def setup_consts(self, vals):
        nc = self.nc
        vals = sorted(set(float(v) for v in vals))
        self.cpool = nc.alloc_sbuf_tensor("cpool", [128, max(1, len(vals))], F32)
        for i, v in enumerate(vals):
            ap = self.cpool.ap()[:, i:i + 1]
            nc.vector.memset(ap, v)
            nc.const_aps.register(F32, v, ap)
        nc.all_engine_barrier()

    def finish(self):
        nc = self.nc
        for s in list(self._sems.values()) + [self.pe, self.act, self.dve, self.pool]:
            s.wait(nc.sync)
        nc.all_engine_barrier()


def emit_qkvz_l0(nc, cx, xg, wqkvz, q_fm, k_fm, v_tm, z_fm, cos_t, sin_t):
    FT = 896
    TT = 512
    NT = T // TT
    with (
        nc.sbuf_tensor("w_sb", [128, KC * FT], BF16) as w_sb,
        nc.sbuf_tensor("cos_sb", [64, SEQ], F32) as cos_sb,
        nc.sbuf_tensor("sin_sb", [64, SEQ], F32) as sin_sb,
        nc.sbuf_tensor("x_sb", [128, 2 * KC * TT], BF16) as x_sb,
        nc.sbuf_tensor("ev_sb", [128, TT], F32) as ev_sb,
        nc.sbuf_tensor("rt1", [64, TT], F32) as rt1,
        nc.sbuf_tensor("rt2", [64, TT], F32) as rt2,
        nc.sbuf_tensor("x2c", [64, TT], F32) as x2c,
        nc.sbuf_tensor("qk_out", [128, 2 * TT], BF16) as qk_out,
        nc.sbuf_tensor("vb_sb", [128, TT], BF16) as vb_sb,
        nc.sbuf_tensor("vt_sb", [128, 4 * 128], BF16) as vt_sb,
        nc.sbuf_tensor("z_out", [128, TT], BF16) as z_out,
        nc.sbuf_tensor("idb", [128, 128], BF16) as idb,
        nc.psum_tensor("pp0", [128, TT], F32) as pp0,
        nc.psum_tensor("pp1", [128, TT], F32) as pp1,
        nc.psum_tensor("pt0", [128, 128], BF16) as pt0,
        nc.psum_tensor("pt1", [128, 128], BF16) as pt1,
    ):
        s_wld = cx.sem("q_wld")
        s_x = [cx.sem("q_x0"), cx.sem("q_x1")]
        s_stq = [cx.sem("q_stq0"), cx.sem("q_stq1")]
        s_stv = [cx.sem("q_stv%d" % i) for i in range(4)]
        s_stz = cx.sem("q_stz")

        ident = nc.inline_tensor(np.eye(128, dtype=np.float32), name="ident_qkvz")
        for c in range(KC):
            cx.dma(nc.gpsimd, w_sb[:, c * FT:(c + 1) * FT],
                   wqkvz[c * 128:(c + 1) * 128, :], s_wld)
        cx.dma(nc.gpsimd, idb[:, :], ident.ap(), s_wld)
        cx.dma(nc.sync, cos_sb[:, :], cos_t, s_wld)
        cx.dma(nc.sync, sin_sb[:, :], sin_t, s_wld)
        pp = [pp0, pp1]
        pt = [pt0, pt1]

        ev_xfree = [None, None]
        ev_pfree = [None, None]
        ev_tfree = [None, None]
        ev_vb = None
        ti = 0
        pi = 0
        tri = 0
        for s in range(NCORES):
            pos0 = (s % 2) * T
            for tt in range(NT):
                slot = ti % 2
                wait_ev(nc.gpsimd, ev_xfree[slot])
                for c in range(KC):
                    ev_x = cx.dma(nc.gpsimd,
                                  x_sb[:, (slot * KC + c) * TT:(slot * KC + c + 1) * TT],
                                  xg[s * D + c * 128: s * D + c * 128 + 128,
                                     tt * TT:(tt + 1) * TT], s_x[slot])
                col0 = s * T + tt * TT

                for f in range(7):
                    pslot = pi % 2
                    s_wld.wait(nc.tensor)
                    wait_ev(nc.tensor, ev_x)
                    wait_ev(nc.tensor, ev_pfree[pslot])
                    for c in range(KC):
                        mm = nc.tensor.matmul(
                            pp[pslot][:, :],
                            w_sb[:, c * FT + f * 128: c * FT + (f + 1) * 128],
                            x_sb[:, (slot * KC + c) * TT:(slot * KC + c + 1) * TT],
                            start=(c == 0), stop=(c == KC - 1))
                    ev_mm = cx.pe.inc(mm)
                    if f == 6:
                        ev_xfree[slot] = ev_mm

                    if f < 4:
                        wait_ev(nc.scalar, ev_mm)
                        cx.dve.wait(nc.scalar)
                        a = nc.scalar.activation(ev_sb[:, :], pp[pslot][:, :], AF.Silu)
                        ev_pfree[pslot] = cx.act.inc(a)
                        a2 = nc.scalar.copy(x2c[:, :], ev_sb[64:128, :])
                        ev_rot_in = cx.act.inc(a2)
                        wait_ev(nc.vector, ev_rot_in)
                        qs = pi % 2
                        qo = qk_out[:, qs * TT:(qs + 1) * TT]
                        s_stq[qs].wait(nc.vector)
                        x1 = ev_sb[0:64, :]
                        csl = cos_sb[:, pos0 + tt * TT: pos0 + (tt + 1) * TT]
                        ssl = sin_sb[:, pos0 + tt * TT: pos0 + (tt + 1) * TT]
                        nc.vector.tensor_tensor(rt1[:, :], x1, csl, ALU.mult)
                        nc.vector.tensor_tensor(rt2[:, :], x2c[:, :], ssl, ALU.mult)
                        nc.vector.tensor_tensor(qo[0:64, :], rt1[:, :], rt2[:, :], ALU.subtract)
                        nc.vector.tensor_tensor(rt1[:, :], x1, ssl, ALU.mult)
                        nc.vector.tensor_tensor(rt2[:, :], x2c[:, :], csl, ALU.mult)
                        v = nc.vector.tensor_tensor(qo[64:128, :], rt1[:, :], rt2[:, :], ALU.add)
                        ev_qk = cx.dve.inc(v)
                        wait_ev(nc.sync, ev_qk)
                        dst = q_fm if f < 2 else k_fm
                        hs = f % 2
                        cx.dma(nc.sync, dst[hs * 128:(hs + 1) * 128, col0:col0 + TT],
                               qo, s_stq[qs])
                    elif f < 6:
                        hs = f - 4
                        wait_ev(nc.vector, ev_mm)
                        wait_ev(nc.vector, ev_vb)
                        v = nc.vector.tensor_copy(vb_sb[:, :], pp[pslot][:, :])
                        ev_vbready = cx.dve.inc(v)
                        ev_pfree[pslot] = ev_vbready
                        for i in range(4):
                            tslot = tri % 2
                            wait_ev(nc.tensor, ev_vbready)
                            wait_ev(nc.tensor, ev_tfree[tslot])
                            tr = nc.tensor.transpose(pt[tslot][:, :],
                                                     vb_sb[:, i * 128:(i + 1) * 128],
                                                     idb[:, :])
                            ev_tr = cx.pe.inc(tr)
                            if i == 3:
                                ev_vb = ev_tr
                            wait_ev(nc.vector, ev_tr)
                            vvs = tri % 4
                            s_stv[vvs].wait(nc.vector)
                            e = nc.vector.tensor_copy(vt_sb[:, vvs * 128:(vvs + 1) * 128],
                                                      pt[tslot][:, :])
                            ev_tfree[tslot] = cx.dve.inc(e)
                            wait_ev(nc.sync, ev_tfree[tslot])
                            r0 = col0 + i * 128
                            cx.dma(nc.sync, v_tm[r0:r0 + 128, hs * 128:(hs + 1) * 128],
                                   vt_sb[:, vvs * 128:(vvs + 1) * 128], s_stv[vvs])
                            tri += 1
                    else:
                        wait_ev(nc.vector, ev_mm)
                        s_stz.wait(nc.vector)
                        v = nc.vector.tensor_copy(z_out[:, :], pp[pslot][:, :])
                        ev_pfree[pslot] = cx.dve.inc(v)
                        wait_ev(nc.sync, ev_pfree[pslot])
                        cx.dma(nc.sync, z_fm[:, col0:col0 + TT], z_out[:, :], s_stz)
                    pi += 1
                ti += 1


def emit_lightning(nc, cx, q_fm, k_fm, v_tm, z_fm, o_hat,
                   dt_c, dq_c, dk_c, lamc_c, wg2_c, nw_c):
    with ExitStack() as est:
        sb = lambda n, sh, dt: est.enter_context(nc.sbuf_tensor(n, sh, dt))
        psb = lambda n, sh, dt=F32: est.enter_context(nc.psum_tensor(n, sh, dt))
        lq = sb("lq", [128, SEQ], BF16)
        lk = sb("lk", [128, SEQ], BF16)
        lv = sb("lv", [128, NCH * 128], BF16)
        lz = sb("lz", [128, TG], BF16)
        lo = sb("lo", [128, SEQ], F32)
        ldt = sb("ldt", [128, HPC * 128], F32)
        ldq = sb("ldq", [128, HPC * 128], BF16)
        ldk = sb("ldk", [128, HPC], F32)
        llam = sb("llam", [128, HPC], F32)
        lwg2 = sb("lwg2", [128, 256], BF16)
        lnw = sb("lnw", [128, HPC], F32)
        lS = sb("lS", [128, 128], F32)
        lSb = sb("lSb", [128, 128], BF16)
        lkt = sb("lkt", [128, 128], BF16)
        lpT = sb("lpT", [128, 128], BF16)
        lqt = sb("lqt", [128, 128], BF16)
        lsg = sb("lsg", [128, 512], F32)
        lxo = sb("lxo", [128, 512], F32)
        lsq = sb("lsq", [128, 512], F32)
        lsr = sb("lsr", [1, 512], F32)
        lones = sb("lones", [128, 1], F32)
        lones1 = sb("lones1", [1, 128], F32)
        lob = sb("lob", [128, 512], BF16)
        lidb = sb("lidb", [128, 128], BF16)
        lps = psb("lps", [128, 128])
        lpo = psb("lpo", [128, 128])
        lpS = psb("lpS", [128, 128])
        lptk = psb("lptk", [128, 128], BF16)
        lpg = psb("lpg", [128, 512])
        lpm = psb("lpm", [1, 512])
        lpb = psb("lpb", [128, 512])
        s_cld = cx.sem("l_cld")
        s_qkv = cx.sem("l_qkv")
        s_sto = cx.sem("l_sto")

        ident = nc.inline_tensor(np.eye(128, dtype=np.float32), name="ident_lgt")
        cx.dma(nc.gpsimd, lidb[:, :], ident.ap(), s_cld)
        for h in range(HPC):
            hp = slice(h * 128, (h + 1) * 128)
            cx.dma(nc.sync, ldt[:, h * 128:(h + 1) * 128], dt_c[hp, :], s_cld)
            cx.dma(nc.gpsimd, ldq[:, h * 128:(h + 1) * 128], dq_c[hp, :], s_cld)
            cx.dma(nc.sync, ldk[:, h:h + 1], dk_c[hp, :], s_cld)
            cx.dma(nc.sync, llam[:, h:h + 1], lamc_c[hp, :], s_cld)
            cx.dma(nc.sync, lnw[:, h:h + 1], nw_c[hp, :], s_cld)
        cx.dma(nc.gpsimd, lwg2[:, :], wg2_c, s_cld)
        for i in range(8):
            cx.dma(nc.sync, lz[:, i * T:(i + 1) * T], z_fm[:, i * T:(i + 1) * T], s_cld)
        nc.vector.memset(lones[:, :], 1.0)
        mset = cx.dve.inc(nc.vector.memset(lones1[:, :], 1.0))

        ev_qkv_user = None
        for b in range(NB):
            bc = b * SEQ
            for hs in range(HPC):
                fr = hs * 128
                wait_ev(nc.sync, ev_qkv_user)
                cx.dma(nc.sync, lq[:, :], q_fm[fr:fr + 128, bc:bc + SEQ], s_qkv)
                cx.dma(nc.sync, lk[:, :], k_fm[fr:fr + 128, bc:bc + SEQ], s_qkv)
                ev_ld = cx.dma(
                    nc.sync, lv[:, :].rearrange("p (c d) -> p c d", c=NCH),
                    v_tm[bc:bc + SEQ, fr:fr + 128].rearrange("(c p) d -> p c d", p=128),
                    s_qkv)
                wait_ev(nc.vector, ev_qkv_user)
                nc.vector.memset(lS[:, :], 0.0)
                ev_S = cx.dve.inc(nc.vector.memset(lSb[:, :], 0.0))

                ev_kt_used = None
                ev_pT_used = None
                ev_qt_used = None
                ev_kt_ready = None
                ev_o = {}
                for c in range(NCH):
                    cc = c * 128
                    s_cld.wait(nc.tensor)
                    s_qkv.wait(nc.tensor)
                    wait_ev(nc.tensor, ev_kt_ready)
                    tr = nc.tensor.transpose(lptk[:, :], lk[:, cc:cc + 128], lidb[:, :])
                    ev_tr = cx.pe.inc(tr)
                    mm1 = nc.tensor.matmul(lps[:, :], lk[:, cc:cc + 128],
                                           lq[:, cc:cc + 128], start=True, stop=True)
                    ev_mm1 = cx.pe.inc(mm1)
                    wait_ev(nc.vector, ev_tr)
                    wait_ev(nc.vector, ev_kt_used)
                    e = nc.vector.tensor_scalar(lkt[:, :], lptk[:, :],
                                                ldk[:, hs:hs + 1], None, ALU.mult)
                    ev_kt_ready = cx.dve.inc(e)
                    wait_ev(nc.vector, ev_mm1)
                    wait_ev(nc.vector, ev_pT_used)
                    e = nc.vector.tensor_tensor(lpT[:, :], lps[:, :],
                                                ldt[:, hs * 128:(hs + 1) * 128], ALU.mult)
                    ev_pT = cx.dve.inc(e)
                    wait_ev(nc.vector, ev_qt_used)
                    e = nc.vector.tensor_tensor(lqt[:, :], lq[:, cc:cc + 128],
                                                ldq[:, hs * 128:(hs + 1) * 128], ALU.mult)
                    ev_qt = cx.dve.inc(e)
                    if c == 0:
                        wait_ev(nc.vector, mset)
                    wait_ev(nc.tensor, ev_pT)
                    wait_ev(nc.tensor, ev_qt)
                    wait_ev(nc.tensor, ev_S)
                    wait_ev(nc.tensor, ev_o.get(c - 1))
                    mmo = nc.tensor.matmul(lpo[:, :], lv[:, cc:cc + 128], lpT[:, :],
                                           start=True, stop=False)
                    ev_pT_used = cx.pe.inc(mmo)
                    mmo2 = nc.tensor.matmul(lpo[:, :], lSb[:, :], lqt[:, :],
                                            start=False, stop=True)
                    ev_qt_used = cx.pe.inc(mmo2)
                    wait_ev(nc.tensor, ev_kt_ready)
                    mmS = nc.tensor.matmul(lpS[:, :], lkt[:, :], lv[:, cc:cc + 128],
                                           start=True, stop=True)
                    ev_kt_used = cx.pe.inc(mmS)
                    if c == NCH - 1:
                        ev_qkv_user = ev_kt_used
                    wait_ev(nc.vector, ev_kt_used)
                    nc.vector.scalar_tensor_tensor(lS[:, :], lS[:, :],
                                                   llam[:, hs:hs + 1], lpS[:, :],
                                                   ALU.mult, ALU.add)
                    ev_S = cx.dve.inc(nc.vector.tensor_copy(lSb[:, :], lS[:, :]))
                    wait_ev(nc.vector, ev_qt_used)
                    ev_o[c] = cx.dve.inc(
                        nc.vector.tensor_copy(lo[:, cc:cc + 128], lpo[:, :]))

                ev_gate_free = None
                for tt in range(8):
                    tc = tt * 512
                    s_cld.wait(nc.tensor)
                    wait_ev(nc.tensor, ev_gate_free)
                    g = nc.tensor.matmul(lpg[:, :], lwg2[:, fr:fr + 128],
                                         lz[:, bc + tc: bc + tc + 512],
                                         start=True, stop=True)
                    ev_g = cx.pe.inc(g)
                    wait_ev(nc.scalar, ev_g)
                    a = nc.scalar.activation(lsg[:, :], lpg[:, :], AF.Sigmoid)
                    ev_sg = cx.act.inc(a)
                    ev_gate_free = ev_sg
                    wait_ev(nc.vector, ev_sg)
                    wait_ev(nc.vector, ev_o.get(NCH - 1))
                    e = nc.vector.tensor_tensor(lxo[:, :], lo[:, tc:tc + 512],
                                                lsg[:, :], ALU.mult)
                    ev_xo = cx.dve.inc(e)
                    wait_ev(nc.scalar, ev_xo)
                    a = nc.scalar.activation(lsq[:, :], lxo[:, :], AF.Square)
                    ev_sq = cx.act.inc(a)
                    wait_ev(nc.tensor, ev_sq)
                    m = nc.tensor.matmul(lpm[:, :], lones[:, :], lsq[:, :],
                                         start=True, stop=True)
                    ev_m = cx.pe.inc(m)
                    wait_ev(nc.scalar, ev_m)
                    a = nc.scalar.activation(lsr[:, :], lpm[:, :], AF.Sqrt,
                                             bias=EPS_RMS, scale=1.0 / 128.0)
                    ev_sr = cx.act.inc(a)
                    wait_ev(nc.vector, ev_sr)
                    e = nc.vector.reciprocal(lsr[:, :], lsr[:, :])
                    ev_sr2 = cx.dve.inc(e)
                    wait_ev(nc.tensor, ev_sr2)
                    bmm = nc.tensor.matmul(lpb[:, :], lones1[:, :], lsr[:, :],
                                           start=True, stop=True)
                    ev_b = cx.pe.inc(bmm)
                    wait_ev(nc.vector, ev_b)
                    s_sto.wait(nc.vector)
                    e = nc.vector.scalar_tensor_tensor(lob[:, :], lxo[:, :],
                                                       lnw[:, hs:hs + 1], lpb[:, :],
                                                       ALU.mult, ALU.mult)
                    ev_ob = cx.dve.inc(e)
                    wait_ev(nc.sync, ev_ob)
                    cx.dma(nc.sync, o_hat[fr:fr + 128, bc + tc: bc + tc + 512],
                           lob[:, :], s_sto)


def emit_woln(nc, cx, og, wo, xres_fm, y_fm, y_bf, bias=None, xcol0=0, tag="wA"):
    TT = 512
    NT = T // TT
    with ExitStack() as est:
        sb = lambda n, sh, dt: est.enter_context(nc.sbuf_tensor(tag + n, sh, dt))
        psb = lambda n, sh, dt=F32: est.enter_context(nc.psum_tensor(tag + n, sh, dt))
        w_sb = sb("w", [128, KC * D], BF16)
        og_sb = sb("og", [128, KC * TT], BF16)
        xr_sb = sb("xr", [128, 2 * TT], F32)
        rs_sb = sb("rs", [128, KC * TT], F32)
        sq_sb = sb("sq", [128, TT], F32)
        b_sb = sb("b", [128, KC], F32)
        mrow = sb("mr", [1, TT], F32)
        m2row = sb("m2", [1, TT], F32)
        srow = sb("sr", [1, TT], F32)
        ones = sb("on", [128, 1], F32)
        ones1 = sb("o1", [1, 128], F32)
        y_sb = sb("y", [128, 2 * TT], F32)
        pd = [psb("pd0", [128, TT]), psb("pd1", [128, TT])]
        p_sum = psb("psm", [1, TT])
        p_sq = psb("psq", [1, TT])
        p_bm = psb("pbm", [128, TT])
        p_br = psb("pbr", [128, TT])

        s_wld = cx.sem(tag + "_wld")
        s_og = cx.sem(tag + "_og")
        s_xr = [cx.sem(tag + "_xr0"), cx.sem(tag + "_xr1")]
        s_y = [cx.sem(tag + "_y0"), cx.sem(tag + "_y1")]
        s_cb = cx.sem(tag + "_cb")
        s_cbr = cx.sem(tag + "_cbr")
        cbf_dram = nc.dram_tensor(tag + "_cbf", [8, T], F32).ap()

        for c in range(KC):
            cx.dma(nc.gpsimd, w_sb[:, c * D:(c + 1) * D],
                   wo[c * 128:(c + 1) * 128, :], s_wld)
        if bias is not None:
            for c in range(KC):
                cx.dma(nc.sync, b_sb[:, c:c + 1], bias[c * 128:(c + 1) * 128, :], s_wld)
        nc.vector.memset(ones[:, :], 1.0)
        mset = cx.dve.inc(nc.vector.memset(ones1[:, :], 1.0))

        ev_og_user = None
        ev_xr_user = [None, None]
        ev_pd_free = [None, None]
        ev_stats_user = None
        ev_statsread = None
        ev_m2_prev = None
        ev_bc_user = None
        pdi = 0
        for tt in range(NT):
            wait_ev(nc.sync, ev_og_user)
            og3 = og.rearrange("(c p) t -> p c t", p=128)
            if isinstance(xcol0, int):
                src3 = og3[:, :, xcol0 + tt * TT: xcol0 + (tt + 1) * TT]
            else:
                src3 = og3[:, :, ds(xcol0 + tt * TT, TT)]
            ev_ogl = cx.dma(nc.sync,
                            og_sb[:, :].rearrange("p (c t) -> p c t", c=KC),
                            src3, s_og)
            for dm in range(KC):
                xslot = dm % 2
                wait_ev(nc.gpsimd, ev_xr_user[xslot])
                ev_xl = cx.dma(nc.gpsimd, xr_sb[:, xslot * TT:(xslot + 1) * TT],
                               xres_fm[dm * 128:(dm + 1) * 128, tt * TT:(tt + 1) * TT],
                               s_xr[xslot])
                ps = pdi % 2
                s_wld.wait(nc.tensor)
                wait_ev(nc.tensor, ev_ogl)
                wait_ev(nc.tensor, ev_pd_free[ps])
                for c in range(KC):
                    mm = nc.tensor.matmul(
                        pd[ps][:, :],
                        w_sb[:, c * D + dm * 128: c * D + (dm + 1) * 128],
                        og_sb[:, c * TT:(c + 1) * TT],
                        start=(c == 0), stop=(c == KC - 1))
                ev_mm = cx.pe.inc(mm)
                if dm == KC - 1:
                    ev_og_user = ev_mm
                wait_ev(nc.vector, ev_mm)
                wait_ev(nc.vector, ev_xl)
                wait_ev(nc.vector, ev_stats_user)
                dst = rs_sb[:, dm * TT:(dm + 1) * TT]
                if bias is not None:
                    e = nc.vector.scalar_tensor_tensor(
                        dst, pd[ps][:, :], b_sb[:, dm:dm + 1],
                        xr_sb[:, xslot * TT:(xslot + 1) * TT], ALU.add, ALU.add)
                else:
                    e = nc.vector.tensor_tensor(
                        dst, pd[ps][:, :], xr_sb[:, xslot * TT:(xslot + 1) * TT], ALU.add)
                ev_rs = cx.dve.inc(e)
                ev_pd_free[ps] = ev_rs
                ev_xr_user[xslot] = ev_rs
                wait_ev(nc.scalar, ev_rs)
                wait_ev(nc.scalar, ev_m2_prev)
                a = nc.scalar.activation(sq_sb[:, :], dst, AF.Square)
                ev_sq = cx.act.inc(a)
                wait_ev(nc.tensor, ev_rs)
                if dm == 0:
                    wait_ev(nc.tensor, mset)
                    wait_ev(nc.tensor, ev_statsread)
                m1 = nc.tensor.matmul(p_sum[:, :], ones[:, :], dst,
                                      start=(dm == 0), stop=(dm == KC - 1))
                cx.pe.inc(m1)
                wait_ev(nc.tensor, ev_sq)
                m2 = nc.tensor.matmul(p_sq[:, :], ones[:, :], sq_sb[:, :],
                                      start=(dm == 0), stop=(dm == KC - 1))
                ev_m2 = cx.pe.inc(m2)
                ev_m2_prev = ev_m2
                ev_stats_user = ev_m2 if dm == KC - 1 else None
                pdi += 1
            wait_ev(nc.scalar, ev_m2)
            a = nc.scalar.activation(mrow[:, :], p_sum[:, :], AF.Copy, scale=1.0 / D)
            cx.act.inc(a)
            a = nc.scalar.activation(m2row[:, :], p_sq[:, :], AF.Copy, scale=1.0 / D)
            ev_cp = cx.act.inc(a)
            ev_statsread = ev_cp
            wait_ev(nc.vector, ev_cp)
            nc.vector.tensor_tensor(srow[:, :], mrow[:, :], mrow[:, :], ALU.mult)
            e = nc.vector.tensor_tensor(srow[:, :], m2row[:, :], srow[:, :], ALU.subtract)
            ev_var = cx.dve.inc(e)
            wait_ev(nc.scalar, ev_var)
            a = nc.scalar.activation(srow[:, :], srow[:, :], AF.Sqrt, bias=EPS_LN)
            ev_std = cx.act.inc(a)
            wait_ev(nc.vector, ev_std)
            e = nc.vector.reciprocal(srow[:, :], srow[:, :])
            ev_rstd = cx.dve.inc(e)
            wait_ev(nc.tensor, ev_rstd)
            wait_ev(nc.tensor, ev_bc_user)
            bm = nc.tensor.matmul(p_bm[:, :], ones1[:, :], mrow[:, :], start=True, stop=True)
            cx.pe.inc(bm)
            br = nc.tensor.matmul(p_br[:, :], ones1[:, :], srow[:, :], start=True, stop=True)
            ev_bc = cx.pe.inc(br)
            for dm in range(KC):
                yslot = dm % 2
                wait_ev(nc.vector, ev_bc)
                s_y[yslot].wait(nc.vector)
                ysl = y_sb[:, yslot * TT:(yslot + 1) * TT]
                nc.vector.tensor_tensor(ysl, rs_sb[:, dm * TT:(dm + 1) * TT],
                                        p_bm[:, :], ALU.subtract)
                e = nc.vector.tensor_tensor(ysl, ysl, p_br[:, :], ALU.mult)
                ev_y = cx.dve.inc(e)
                if dm == KC - 1:
                    ev_bc_user = ev_y
                wait_ev(nc.sync, ev_y)
                cx.dma(nc.sync, y_fm[dm * 128:(dm + 1) * 128, tt * TT:(tt + 1) * TT],
                       ysl, s_y[yslot])
                wait_ev(nc.gpsimd, ev_y)
                cx.dma(nc.gpsimd, y_bf[dm * 128:(dm + 1) * 128, tt * TT:(tt + 1) * TT],
                       ysl, s_y[yslot])


def emit_moe(nc, cx, x_bf, x_fm, gate_w, wg_a, wu_a, wd_a, swg, swu, swd,
             y_fm, y_bf, tag="mo", dbg_acc=None):
    TT = 512
    TP = 1024
    NP = T // TP
    NE = 10
    HT = 11
    with ExitStack() as est:
        sb = lambda n, sh, dt: est.enter_context(nc.sbuf_tensor(tag + n, sh, dt))
        psb = lambda n, sh, dt=F32: est.enter_context(nc.psum_tensor(tag + n, sh, dt))
        xs = sb("xs", [128, KC * TP], BF16)
        acc = sb("acc", [128, KC * TP], F32)
        wgu = sb("wgu", [128, 2 * 2 * KC * 128], BF16)
        gh = sb("gh", [128, HT * TP], BF16)
        wdd = sb("wdd", [128, 2 * HT * 128], BF16)
        cbst = sb("cbst", [8, 128], F32)
        cbrow = sb("cbrow", [1, TP], F32)
        cbs = sb("cbs", [128, TP], F32)
        sg = sb("sg", [128, 2 * TT], F32)
        gwsb = sb("gw", [128, KC * 8], F32)
        lgs = sb("lgs", [8, TT], F32)
        lgt = sb("lgt", [128, 8], F32)
        prw = sb("prw", [128, 8], F32)
        rsel = sb("rsel", [128, 8], F32)
        rtmp = sb("rt", [128, 8], F32)
        ridx = sb("ri", [128, 8], F32)
        rm = sb("rm", [128, 1], F32)
        rsum = sb("rsm", [128, 1], F32)
        idf = sb("idf", [128, 128], F32)
        xr_sb = sb("xr", [128, 2 * TT], F32)
        rs_sb = sb("rsb", [128, KC * TT], F32)
        sq_sb = sb("sq", [128, TT], F32)
        mrow = sb("mr", [1, TT], F32)
        m2row = sb("m2", [1, TT], F32)
        srow = sb("srw", [1, TT], F32)
        ones = sb("on", [128, 1], F32)
        ones1 = sb("o1", [1, 128], F32)
        y_sb = sb("y", [128, 2 * TT], F32)
        pg = [psb("pg0", [128, TT]), psb("pg1", [128, TT])]
        pu = [psb("pu0", [128, TT]), psb("pu1", [128, TT])]
        pdn = [psb("pd0", [128, TT]), psb("pd1", [128, TT])]
        prt = pdn[0]
        p_sum = psb("psm", [1, TT])
        p_sq = psb("psq", [1, TT])
        p_bm = pdn[0]
        p_br = pdn[1]
        bcm_sb = sb("bcm", [128, TT], F32)
        bcr_sb = sb("bcr", [128, TT], F32)

        s_wld = cx.sem(tag + "_wld")
        s_x = cx.sem(tag + "_x")
        s_xr = [cx.sem(tag + "_xr0"), cx.sem(tag + "_xr1")]
        s_wgu = [cx.sem(tag + "_wg0"), cx.sem(tag + "_wg1")]
        s_wd = [cx.sem(tag + "_wd0"), cx.sem(tag + "_wd1")]
        s_y = [cx.sem(tag + "_y0"), cx.sem(tag + "_y1")]
        s_cb = cx.sem(tag + "_cb")
        s_cbr = cx.sem(tag + "_cbr")
        cbf_dram = nc.dram_tensor(tag + "_cbf", [8, T], F32).ap()

        identf = nc.inline_tensor(np.eye(128, dtype=np.float32), name="ident_moe")
        cx.dma(nc.sync, idf[:, :], identf.ap(), s_wld)
        ev_iota = cx.pool.inc(nc.gpsimd.iota(
            ridx[:, :], [[1, 8]], base=0, channel_multiplier=0,
            allow_small_or_imprecise_dtypes=True))
        nc.vector.memset(ones[:, :], 1.0)
        mset = cx.dve.inc(nc.vector.memset(ones1[:, :], 1.0))
        for c in range(KC):
            cx.dma(nc.sync, gwsb[:, c * 8:(c + 1) * 8],
                   gate_w[c * 128:(c + 1) * 128, :], s_wld)

        ev_lgs_user = None
        ev_prt_user = None
        ev_xr_user = [None, None]
        ev_lg = None
        ev_exp_prev = None
        for rt in range(T // TT):
            xslot = 0
            s_wld.wait(nc.tensor)
            for c in range(KC):
                xslot = c % 2
                wait_ev(nc.gpsimd, ev_xr_user[xslot])
                ev_xl = cx.dma(nc.gpsimd, xr_sb[:, xslot * TT:(xslot + 1) * TT],
                               x_fm[c * 128:(c + 1) * 128, rt * TT:(rt + 1) * TT],
                               s_xr[xslot])
                wait_ev(nc.tensor, ev_xl)
                if c == 0:
                    wait_ev(nc.tensor, ev_lg)
                mm = nc.tensor.matmul(pg[0][0:8, :], gwsb[:, c * 8:(c + 1) * 8],
                                      xr_sb[:, xslot * TT:(xslot + 1) * TT],
                                      start=(c == 0), stop=(c == KC - 1))
                ev_mm = cx.pe.inc(mm)
                ev_xr_user[xslot] = ev_mm
            wait_ev(nc.vector, ev_mm)
            wait_ev(nc.vector, ev_lgs_user)
            ev_lg = cx.dve.inc(nc.vector.tensor_copy(lgs[:, :], pg[0][0:8, :]))
            for i in range(4):
                wait_ev(nc.tensor, ev_lg)
                wait_ev(nc.tensor, ev_prt_user)
                tr = nc.tensor.transpose(prt[:, 0:8], lgs[:, i * 128:(i + 1) * 128],
                                         idf[0:8, 0:8])
                ev_tr = cx.pe.inc(tr)
                if i == 3:
                    ev_lgs_user = ev_tr
                wait_ev(nc.vector, ev_tr)
                wait_ev(nc.vector, ev_exp_prev)
                e2 = nc.vector.tensor_copy(lgt[:, :], prt[:, 0:8])
                ev_prt_user = cx.dve.inc(e2)
                e2 = nc.vector.tensor_reduce(rm[:, :], lgt[:, :], axis=AX.X,
                                             op=ALU.max, negate=True)
                ev_nm = cx.dve.inc(e2)
                wait_ev(nc.scalar, ev_nm)
                a = nc.scalar.activation(prw[:, :], lgt[:, :], AF.Exp,
                                         bias=rm[:, 0:1], accum_out=rsum[:, 0:1])
                ev_exp = cx.act.inc(a)
                ev_exp_prev = ev_exp
                wait_ev(nc.vector, ev_exp)
                def selfw(op):
                    ev = cx.dve.inc(op)
                    wait_ev(nc.vector, ev)
                selfw(nc.vector.reciprocal(rsum[:, :], rsum[:, :]))
                selfw(nc.vector.tensor_scalar(prw[:, :], prw[:, :], rsum[:, 0:1],
                                              None, ALU.mult))
                if rt == 0 and i == 0:
                    wait_ev(nc.vector, ev_iota)
                selfw(nc.vector.tensor_reduce(rm[:, :], prw[:, :], axis=AX.X, op=ALU.max))
                selfw(nc.vector.tensor_scalar(rtmp[:, :], prw[:, :], rm[:, 0:1],
                                              None, ALU.subtract))
                selfw(nc.vector.tensor_scalar(rsel[:, :], rtmp[:, :], 0.0, None,
                                              ALU.is_ge))
                selfw(nc.vector.tensor_scalar(rtmp[:, :], rsel[:, :], -100.0, 100.0,
                                              ALU.mult, ALU.add))
                selfw(nc.vector.tensor_tensor(rtmp[:, :], rtmp[:, :], ridx[:, :],
                                              ALU.add))
                selfw(nc.vector.tensor_reduce(rm[:, :], rtmp[:, :], axis=AX.X, op=ALU.min))
                selfw(nc.vector.tensor_scalar(rtmp[:, :], ridx[:, :], rm[:, 0:1],
                                              None, ALU.subtract))
                selfw(nc.vector.tensor_scalar(rsel[:, :], rtmp[:, :], 0.0, None,
                                              ALU.is_equal))
                selfw(nc.vector.tensor_scalar(rtmp[:, :], rsel[:, :], 2.0, None,
                                              ALU.mult))
                selfw(nc.vector.tensor_tensor(rtmp[:, :], prw[:, :], rtmp[:, :],
                                              ALU.subtract))
                selfw(nc.vector.tensor_reduce(rm[:, :], rtmp[:, :], axis=AX.X, op=ALU.max))
                selfw(nc.vector.tensor_scalar(rtmp[:, :], rtmp[:, :], rm[:, 0:1],
                                              None, ALU.subtract))
                selfw(nc.vector.tensor_scalar(rtmp[:, :], rtmp[:, :], 0.0, None,
                                              ALU.is_ge))
                selfw(nc.vector.tensor_scalar(rtmp[:, :], rtmp[:, :], -100.0, 100.0,
                                              ALU.mult, ALU.add))
                selfw(nc.vector.tensor_tensor(rtmp[:, :], rtmp[:, :], ridx[:, :],
                                              ALU.add))
                selfw(nc.vector.tensor_reduce(rm[:, :], rtmp[:, :], axis=AX.X, op=ALU.min))
                selfw(nc.vector.tensor_scalar(rtmp[:, :], ridx[:, :], rm[:, 0:1],
                                              None, ALU.subtract))
                selfw(nc.vector.tensor_scalar(rtmp[:, :], rtmp[:, :], 0.0, None,
                                              ALU.is_equal))
                selfw(nc.vector.tensor_tensor(rsel[:, :], rsel[:, :], rtmp[:, :],
                                              ALU.add))
                e2 = nc.vector.tensor_tensor(prw[:, :], prw[:, :], rsel[:, :], ALU.mult)
                ev_cb = cx.dve.inc(e2)
                wait_ev(nc.vector, ev_cb)
                wait_ev(nc.tensor, ev_cb)
                tr = nc.tensor.transpose(prt[0:8, 0:128], prw[:, :], idf[:, :])
                ev_tr2 = cx.pe.inc(tr)
                wait_ev(nc.vector, ev_tr2)
                s_cb.wait(nc.vector)
                e2 = nc.vector.tensor_copy(cbst[:, :], prt[0:8, 0:128])
                ev_prt_user = cx.dve.inc(e2)
                wait_ev(nc.sync, ev_prt_user)
                cx.dma(nc.sync, cbf_dram[:, rt * TT + i * 128: rt * TT + (i + 1) * 128],
                       cbst[:, :], s_cb)
        ev_router_done = ev_prt_user
        ev_pg_free = [ev_lg, None]
        ev_pu_free = [None, None]
        ev_pd_init = ev_prt_user

        ev_x_user = None
        ev_wgu_user = [None, None]
        ev_wdd_user = [None, None]
        ev_pd_free = [ev_pd_init, None]
        ev_acc_user = None
        ev_pbm_user = None
        ev_b2_prev = None
        ev_m2_prev = None
        ev_sg_user = [None, None]
        ev_statsread = None
        ev_down_last = None
        gi = 0
        di = 0
        for p in range(NP):
            pc = p * TP
            wait_ev(nc.sync, ev_x_user)
            for c in range(KC):
                cx.dma(nc.sync, xs[:, c * TP:(c + 1) * TP],
                       x_bf[c * 128:(c + 1) * 128, pc:pc + TP], s_x)
            for e in range(NE):
                if e < 8:
                    wg_src = wg_a[e, :, :]
                    wu_src = wu_a[e, :, :]
                    wd_src = wd_a[e, :, :]
                else:
                    h0 = (e - 8) * 1408
                    wg_src = swg[:, h0:h0 + 1408]
                    wu_src = swu[:, h0:h0 + 1408]
                    wd_src = swd[h0:h0 + 1408, :]
                if e < 8:
                    wait_ev(nc.sync, ev_b2_prev)
                    s_cb.wait(nc.sync)
                    ev_cbl = cx.dma(nc.sync, cbrow[:, :], cbf_dram[e:e + 1, pc:pc + TP],
                                    s_cbr)
                    wait_ev(nc.tensor, ev_cbl)
                    wait_ev(nc.tensor, ev_pbm_user)
                    wait_ev(nc.tensor, ev_pd_free[0])
                    wait_ev(nc.tensor, ev_pd_free[1])
                    wait_ev(nc.tensor, mset)
                    b1 = nc.tensor.matmul(p_bm[:, 0:TT], ones1[:, :],
                                          cbrow[:, 0:TT], start=True, stop=True)
                    cx.pe.inc(b1)
                    b2 = nc.tensor.matmul(p_br[:, 0:TT], ones1[:, :],
                                          cbrow[:, TT:TP], start=True, stop=True)
                    ev_b2 = cx.pe.inc(b2)
                    ev_b2_prev = ev_b2
                    wait_ev(nc.vector, ev_b2)
                    nc.vector.tensor_copy(cbs[:, 0:TT], p_bm[:, 0:TT])
                    ev_pbm_user = cx.dve.inc(
                        nc.vector.tensor_copy(cbs[:, TT:TP], p_br[:, 0:TT]))
                    ev_pd_free[0] = ev_pbm_user
                    ev_pd_free[1] = ev_pbm_user
                for ht in range(HT):
                    wslot = gi % 2
                    wait_ev(nc.gpsimd, ev_wgu_user[wslot])
                    base = wslot * 2 * KC * 128
                    cx.dma(nc.gpsimd,
                           wgu[:, base: base + KC * 128].rearrange(
                               "p (g h) -> p g h", g=KC),
                           wg_src[:, ht * 128:(ht + 1) * 128].rearrange(
                               "(g p) h -> p g h", p=128), s_wgu[wslot])
                    ev_w2 = cx.dma(nc.gpsimd,
                                   wgu[:, base + KC * 128: base + 2 * KC * 128].rearrange(
                                       "p (g h) -> p g h", g=KC),
                                   wu_src[:, ht * 128:(ht + 1) * 128].rearrange(
                                       "(g p) h -> p g h", p=128), s_wgu[wslot])
                    for th in range(TP // TT):
                        tc = th * TT
                        ps2 = th
                        sgs = sg[:, th * TT:(th + 1) * TT]
                        nc.tensor.wait_ge(s_x.h, s_x.n)
                        wait_ev(nc.tensor, ev_w2)
                        wait_ev(nc.tensor, ev_pg_free[ps2])
                        for c in range(KC):
                            mmg = nc.tensor.matmul(
                                pg[ps2][:, :], wgu[:, base + c * 128: base + (c + 1) * 128],
                                xs[:, c * TP + tc: c * TP + tc + TT],
                                start=(c == 0), stop=(c == KC - 1))
                        ev_g = cx.pe.inc(mmg)
                        wait_ev(nc.tensor, ev_pu_free[ps2])
                        for c in range(KC):
                            mmu = nc.tensor.matmul(
                                pu[ps2][:, :],
                                wgu[:, base + (KC + c) * 128: base + (KC + c + 1) * 128],
                                xs[:, c * TP + tc: c * TP + tc + TT],
                                start=(c == 0), stop=(c == KC - 1))
                        ev_u = cx.pe.inc(mmu)
                        if th == TP // TT - 1:
                            ev_wgu_user[wslot] = ev_u
                        wait_ev(nc.scalar, ev_g)
                        wait_ev(nc.scalar, ev_sg_user[ps2])
                        a = nc.scalar.activation(sgs, pg[ps2][:, :], AF.Silu)
                        ev_pg_free[ps2] = cx.act.inc(a)
                        wait_ev(nc.vector, ev_pg_free[ps2])
                        wait_ev(nc.vector, ev_u)
                        if ht == 0 and th == 0:
                            wait_ev(nc.vector, ev_down_last)
                        dst = gh[:, ht * TP + tc: ht * TP + tc + TT]
                        if e < 8:
                            nc.vector.tensor_tensor(sgs, sgs, pu[ps2][:, :], ALU.mult)
                            ev_pu_free[ps2] = cx.dve.inc(
                                nc.vector.tensor_tensor(dst, sgs,
                                                        cbs[:, tc:tc + TT], ALU.mult))
                        else:
                            ev_pu_free[ps2] = cx.dve.inc(
                                nc.vector.tensor_tensor(dst, sgs, pu[ps2][:, :], ALU.mult))
                        ev_sg_user[ps2] = ev_pu_free[ps2]
                        ev_gh = ev_pu_free[ps2]
                    gi += 1
                for dm in range(KC):
                    wslot = di % 2
                    wait_ev(nc.gpsimd, ev_wdd_user[wslot])
                    ev_wd = cx.dma(nc.gpsimd,
                                   wdd[:, wslot * HT * 128:(wslot + 1) * HT * 128].rearrange(
                                       "p (g d) -> p g d", g=HT),
                                   wd_src[:, dm * 128:(dm + 1) * 128].rearrange(
                                       "(g p) d -> p g d", p=128), s_wd[wslot])
                    for th in range(TP // TT):
                        tc = th * TT
                        ps = th
                        wait_ev(nc.tensor, ev_wd)
                        wait_ev(nc.tensor, ev_gh)
                        wait_ev(nc.tensor, ev_pd_free[ps])
                        for ht in range(HT):
                            mmd = nc.tensor.matmul(
                                pdn[ps][:, :],
                                wdd[:, wslot * HT * 128 + ht * 128:
                                    wslot * HT * 128 + (ht + 1) * 128],
                                gh[:, ht * TP + tc: ht * TP + tc + TT],
                                start=(ht == 0), stop=(ht == HT - 1))
                        ev_d = cx.pe.inc(mmd)
                        if th == TP // TT - 1:
                            ev_wdd_user[wslot] = ev_d
                        wait_ev(nc.vector, ev_d)
                        wait_ev(nc.vector, ev_acc_user)
                        asl = acc[:, dm * TP + tc: dm * TP + tc + TT]
                        if e == 0:
                            ev_pd_free[ps] = cx.dve.inc(
                                nc.vector.tensor_copy(asl, pdn[ps][:, :]))
                        else:
                            ev_pd_free[ps] = cx.dve.inc(
                                nc.vector.tensor_tensor(asl, pdn[ps][:, :], asl, ALU.add))
                    di += 1
                ev_down_last = ev_d
            ev_x_user = ev_d
            if dbg_acc is not None:
                s_dbga = cx.sem(tag + "_dbga")
                for dm in range(KC):
                    wait_ev(nc.sync, ev_pd_free[0])
                    wait_ev(nc.sync, ev_pd_free[1])
                    cx.dma(nc.sync, dbg_acc[dm * 128:(dm + 1) * 128, pc:pc + TP],
                           acc[:, dm * TP:(dm + 1) * TP], s_dbga)

            ev_xr2_user = [None, None]
            for th in range(TP // TT):
                tc = th * TT
                for dm in range(KC):
                    xslot = dm % 2
                    wait_ev(nc.gpsimd, ev_xr2_user[xslot])
                    ev_xl = cx.dma(nc.gpsimd, xr_sb[:, xslot * TT:(xslot + 1) * TT],
                                   x_fm[dm * 128:(dm + 1) * 128, pc + tc: pc + tc + TT],
                                   s_xr[xslot])
                    wait_ev(nc.vector, ev_xl)
                    if th == 0 and dm == 0:
                        wait_ev(nc.vector, ev_pd_free[0])
                        wait_ev(nc.vector, ev_pd_free[1])
                    dst = rs_sb[:, dm * TT:(dm + 1) * TT]
                    e2 = nc.vector.tensor_tensor(dst, acc[:, dm * TP + tc: dm * TP + tc + TT],
                                                 xr_sb[:, xslot * TT:(xslot + 1) * TT],
                                                 ALU.add)
                    ev_rs = cx.dve.inc(e2)
                    ev_xr2_user[xslot] = ev_rs
                    wait_ev(nc.scalar, ev_rs)
                    wait_ev(nc.scalar, ev_m2_prev)
                    a = nc.scalar.activation(sq_sb[:, :], dst, AF.Square)
                    ev_sq = cx.act.inc(a)
                    wait_ev(nc.tensor, ev_rs)
                    if dm == 0:
                        wait_ev(nc.tensor, mset)
                        wait_ev(nc.tensor, ev_statsread)
                    nc.tensor.matmul(p_sum[:, :], ones[:, :], dst,
                                     start=(dm == 0), stop=(dm == KC - 1))
                    wait_ev(nc.tensor, ev_sq)
                    m2 = nc.tensor.matmul(p_sq[:, :], ones[:, :], sq_sb[:, :],
                                          start=(dm == 0), stop=(dm == KC - 1))
                    ev_m2 = cx.pe.inc(m2)
                    ev_m2_prev = ev_m2
                wait_ev(nc.scalar, ev_m2)
                a = nc.scalar.activation(mrow[:, :], p_sum[:, :], AF.Copy, scale=1.0 / D)
                cx.act.inc(a)
                a = nc.scalar.activation(m2row[:, :], p_sq[:, :], AF.Copy, scale=1.0 / D)
                ev_cp = cx.act.inc(a)
                ev_statsread = ev_cp
                wait_ev(nc.vector, ev_cp)
                nc.vector.tensor_tensor(srow[:, :], mrow[:, :], mrow[:, :], ALU.mult)
                e2 = nc.vector.tensor_tensor(srow[:, :], m2row[:, :], srow[:, :],
                                             ALU.subtract)
                ev_var = cx.dve.inc(e2)
                wait_ev(nc.scalar, ev_var)
                a = nc.scalar.activation(srow[:, :], srow[:, :], AF.Sqrt, bias=EPS_LN)
                ev_std = cx.act.inc(a)
                wait_ev(nc.vector, ev_std)
                e2 = nc.vector.reciprocal(srow[:, :], srow[:, :])
                ev_rstd = cx.dve.inc(e2)
                wait_ev(nc.tensor, ev_rstd)
                wait_ev(nc.tensor, ev_pbm_user)
                wait_ev(nc.tensor, ev_pd_free[0])
                wait_ev(nc.tensor, ev_pd_free[1])
                bm = nc.tensor.matmul(p_bm[:, 0:TT], ones1[:, :], mrow[:, :],
                                      start=True, stop=True)
                cx.pe.inc(bm)
                br = nc.tensor.matmul(p_br[:, 0:TT], ones1[:, :], srow[:, :],
                                      start=True, stop=True)
                ev_bc = cx.pe.inc(br)
                wait_ev(nc.vector, ev_bc)
                nc.vector.tensor_copy(bcm_sb[:, :], p_bm[:, 0:TT])
                e2 = nc.vector.tensor_copy(bcr_sb[:, :], p_br[:, 0:TT])
                ev_bcc = cx.dve.inc(e2)
                ev_pd_free[0] = ev_bcc
                ev_pd_free[1] = ev_bcc
                for dm in range(KC):
                    yslot = dm % 2
                    wait_ev(nc.vector, ev_bcc)
                    s_y[yslot].wait(nc.vector)
                    ysl = y_sb[:, yslot * TT:(yslot + 1) * TT]
                    nc.vector.tensor_tensor(ysl, rs_sb[:, dm * TT:(dm + 1) * TT],
                                            bcm_sb[:, :], ALU.subtract)
                    e2 = nc.vector.tensor_tensor(ysl, ysl, bcr_sb[:, :], ALU.mult)
                    ev_y = cx.dve.inc(e2)
                    if dm == KC - 1:
                        ev_pbm_user = ev_y
                    wait_ev(nc.sync, ev_y)
                    cx.dma(nc.sync, y_fm[dm * 128:(dm + 1) * 128, pc + tc: pc + tc + TT],
                           ysl, s_y[yslot])
                    wait_ev(nc.gpsimd, ev_y)
                    cx.dma(nc.gpsimd, y_bf[dm * 128:(dm + 1) * 128, pc + tc: pc + tc + TT],
                           ysl, s_y[yslot])
                ev_acc_user = ev_y
    return cbf_dram


def emit_qkv_l1(nc, cx, xg, wqkv, bqkv, q_fm, k_fm, v_tm, tag="j"):
    FT = 768
    TT = 512
    NT = T // TT
    SCL = float(1.0 / np.sqrt(128.0))
    with ExitStack() as est:
        sb = lambda n, sh, dt: est.enter_context(nc.sbuf_tensor(tag + n, sh, dt))
        psb = lambda n, sh, dt=F32: est.enter_context(nc.psum_tensor(tag + n, sh, dt))
        w_sb = sb("w", [128, KC * FT], BF16)
        b_sb = sb("b", [128, 6], F32)
        x_sb = sb("x", [128, 2 * KC * TT], BF16)
        ev_sb = sb("ev", [128, TT], F32)
        qk_out = sb("qk", [128, 2 * TT], BF16)
        vb_sb = sb("vb", [128, TT], BF16)
        vt_sb = sb("vt", [128, 4 * 128], BF16)
        idb = sb("id", [128, 128], BF16)
        pp = [psb("p0", [128, TT]), psb("p1", [128, TT])]
        pt = [psb("t0", [128, 128], BF16), psb("t1", [128, 128], BF16)]

        s_wld = cx.sem(tag + "_wld")
        s_x = [cx.sem(tag + "_x0"), cx.sem(tag + "_x1")]
        s_stq = [cx.sem(tag + "_stq0"), cx.sem(tag + "_stq1")]
        s_stv = [cx.sem(tag + "_stv%d" % i) for i in range(4)]

        ident = nc.inline_tensor(np.eye(128, dtype=np.float32), name="ident_" + tag)
        for c in range(KC):
            cx.dma(nc.gpsimd, w_sb[:, c * FT:(c + 1) * FT],
                   wqkv[c * 128:(c + 1) * 128, :], s_wld)
        for f in range(6):
            cx.dma(nc.sync, b_sb[:, f:f + 1], bqkv[f * 128:(f + 1) * 128, :], s_wld)
        cx.dma(nc.gpsimd, idb[:, :], ident.ap(), s_wld)

        ev_xfree = [None, None]
        ev_pfree = [None, None]
        ev_tfree = [None, None]
        ev_vb = None
        ti = 0
        pi = 0
        tri = 0
        for s in range(NCORES):
            for tt in range(NT):
                slot = ti % 2
                wait_ev(nc.gpsimd, ev_xfree[slot])
                for c in range(KC):
                    ev_x = cx.dma(nc.gpsimd,
                                  x_sb[:, (slot * KC + c) * TT:(slot * KC + c + 1) * TT],
                                  xg[s * D + c * 128: s * D + c * 128 + 128,
                                     tt * TT:(tt + 1) * TT], s_x[slot])
                col0 = s * T + tt * TT
                for f in range(6):
                    pslot = pi % 2
                    s_wld.wait(nc.tensor)
                    wait_ev(nc.tensor, ev_x)
                    wait_ev(nc.tensor, ev_pfree[pslot])
                    for c in range(KC):
                        mm = nc.tensor.matmul(
                            pp[pslot][:, :],
                            w_sb[:, c * FT + f * 128: c * FT + (f + 1) * 128],
                            x_sb[:, (slot * KC + c) * TT:(slot * KC + c + 1) * TT],
                            start=(c == 0), stop=(c == KC - 1))
                    ev_mm = cx.pe.inc(mm)
                    if f == 5:
                        ev_xfree[slot] = ev_mm

                    if f < 4:
                        hs = f % 2
                        wait_ev(nc.vector, ev_mm)
                        qs = pi % 2
                        qo = qk_out[:, qs * TT:(qs + 1) * TT]
                        s_stq[qs].wait(nc.vector)
                        if f < 2:
                            nc.vector.tensor_scalar(ev_sb[:, :], pp[pslot][:, :],
                                                    b_sb[:, f:f + 1], None, ALU.add)
                            e = nc.vector.tensor_scalar(qo, ev_sb[:, :],
                                                        SCL, None, ALU.mult)
                        else:
                            e = nc.vector.tensor_scalar(qo, pp[pslot][:, :],
                                                        b_sb[:, f:f + 1], None, ALU.add)
                        ev_qk = cx.dve.inc(e)
                        ev_pfree[pslot] = ev_qk
                        wait_ev(nc.sync, ev_qk)
                        dst = q_fm if f < 2 else k_fm
                        cx.dma(nc.sync, dst[hs * 128:(hs + 1) * 128, col0:col0 + TT],
                               qo, s_stq[qs])
                    else:
                        hs = f - 4
                        wait_ev(nc.vector, ev_mm)
                        wait_ev(nc.vector, ev_vb)
                        v = nc.vector.tensor_scalar(vb_sb[:, :], pp[pslot][:, :],
                                                    b_sb[:, f:f + 1], None, ALU.add)
                        ev_vbready = cx.dve.inc(v)
                        ev_pfree[pslot] = ev_vbready
                        for i in range(4):
                            tslot = tri % 2
                            wait_ev(nc.tensor, ev_vbready)
                            wait_ev(nc.tensor, ev_tfree[tslot])
                            tr = nc.tensor.transpose(pt[tslot][:, :],
                                                     vb_sb[:, i * 128:(i + 1) * 128],
                                                     idb[:, :])
                            ev_tr = cx.pe.inc(tr)
                            if i == 3:
                                ev_vb = ev_tr
                            wait_ev(nc.vector, ev_tr)
                            vvs = tri % 4
                            s_stv[vvs].wait(nc.vector)
                            e = nc.vector.tensor_copy(vt_sb[:, vvs * 128:(vvs + 1) * 128],
                                                      pt[tslot][:, :])
                            ev_tfree[tslot] = cx.dve.inc(e)
                            wait_ev(nc.sync, ev_tfree[tslot])
                            r0 = col0 + i * 128
                            cx.dma(nc.sync, v_tm[r0:r0 + 128, hs * 128:(hs + 1) * 128],
                                   vt_sb[:, vvs * 128:(vvs + 1) * 128], s_stv[vvs])
                            tri += 1
                    pi += 1
                ti += 1


def causal_masks():
    out = np.zeros((4, 128, 512), np.float32)
    j = np.arange(512)
    tq = np.arange(128)
    for m in range(4):
        out[m] = np.where(j[None, :] <= m * 128 + tq[:, None], 0.0, -1.0e5)
    return out


def emit_causal(nc, cx, q_fm, k_fm, v_tm, o_hat, cmask, tag="c"):
    with ExitStack() as est:
        sb = lambda n, sh, dt: est.enter_context(nc.sbuf_tensor(tag + n, sh, dt))
        psb = lambda n, sh, dt=F32: est.enter_context(nc.psum_tensor(tag + n, sh, dt))
        lq = sb("lq", [128, SEQ], BF16)
        lk = sb("lk", [128, SEQ], BF16)
        lv = sb("lv", [128, NCH * 128], BF16)
        srow = sb("srow", [128, 2 * SEQ], F32)
        p_sb = sb("p", [128, 2 * SEQ], BF16)
        msk = sb("msk", [128, 4 * 512], F32)
        nm = sb("nm", [128, 2], F32)
        rsum = sb("rsum", [128, 2], F32)
        rtr = sb("rtr", [1, 2 * 128], F32)
        bc_sb = sb("bc", [128, 2 * 128], F32)
        pT_sb = sb("pT", [128, 4 * 128], BF16)
        o_bf = sb("ob", [128, 2 * 128], BF16)
        idb = sb("id", [128, 128], BF16)
        idf = sb("idf", [128, 128], F32)
        ones1 = sb("o1", [1, 128], F32)
        ps_sc = [psb("s0", [128, 512]), psb("s1", [128, 512])]
        pt_ps = [psb("t0", [128, 128], BF16), psb("t1", [128, 128], BF16)]
        po = [psb("po0", [128, 128]), psb("po1", [128, 128])]
        pb = psb("pb", [128, 128])

        s_cld = cx.sem(tag + "_cld")
        s_qkv = cx.sem(tag + "_qkv")
        s_sto = [cx.sem(tag + "_so0"), cx.sem(tag + "_so1")]

        identb = nc.inline_tensor(np.eye(128, dtype=np.float32), name="identb_" + tag)
        cx.dma(nc.gpsimd, idb[:, :], identb.ap(), s_cld)
        cx.dma(nc.sync, idf[:, :], identb.ap(), s_cld)
        for m in range(4):
            cx.dma(nc.sync, msk[:, m * 512:(m + 1) * 512],
                   cmask[m * 128:(m + 1) * 128, :], s_cld)
        mset = cx.dve.inc(nc.vector.memset(ones1[:, :], 1.0))

        ev_qkv_user = None
        for b in range(NB):
            bc = b * SEQ
            for hs in range(HPC):
                fr = hs * 128
                wait_ev(nc.sync, ev_qkv_user)
                cx.dma(nc.sync, lq[:, :], q_fm[fr:fr + 128, bc:bc + SEQ], s_qkv)
                cx.dma(nc.sync, lk[:, :], k_fm[fr:fr + 128, bc:bc + SEQ], s_qkv)
                cx.dma(nc.sync, lv[:, :].rearrange("p (c d) -> p c d", c=NCH),
                       v_tm[bc:bc + SEQ, fr:fr + 128].rearrange("(c p) d -> p c d", p=128),
                       s_qkv)

                ev_sc_user = [None, None]
                ev_exp = [None, None]
                ev_avdone = [None, None]
                ev_pT_user = [None, None, None, None]
                ev_pt_free = [None, None]
                ev_bc = [None, None]
                ev_pb = [None]
                ev_o = [None, None]
                sci = 0
                tri = 0

                def emit_scores_softmax(c):
                    nonlocal sci
                    sl = c % 2
                    ng = c // 4 + 1
                    E = ng * 512
                    sro = srow[:, sl * SEQ: sl * SEQ + SEQ]
                    for g in range(ng):
                        ss = sci % 2
                        s_cld.wait(nc.tensor)
                        s_qkv.wait(nc.tensor)
                        wait_ev(nc.tensor, ev_sc_user[ss])
                        mm = nc.tensor.matmul(ps_sc[ss][:, :], lq[:, c * 128:(c + 1) * 128],
                                              lk[:, g * 512:(g + 1) * 512],
                                              start=True, stop=True)
                        ev_mm = cx.pe.inc(mm)
                        if g == ng - 1:
                            wait_ev(nc.vector, ev_mm)
                            wait_ev(nc.vector, ev_exp[sl])
                            e = nc.vector.tensor_tensor(
                                sro[:, g * 512:(g + 1) * 512], ps_sc[ss][:, :],
                                msk[:, (c % 4) * 512:(c % 4 + 1) * 512], ALU.add)
                            ev_sc_user[ss] = cx.dve.inc(e)
                        else:
                            wait_ev(nc.scalar, ev_mm)
                            if g == 0:
                                wait_ev(nc.scalar, ev_exp[sl])
                            a = nc.scalar.activation(sro[:, g * 512:(g + 1) * 512],
                                                     ps_sc[ss][:, :], AF.Copy)
                            ev_sc_user[ss] = cx.act.inc(a)
                        sci += 1
                    wait_ev(nc.vector, ev_sc_user[(sci - 1) % 2])
                    if ng > 1:
                        wait_ev(nc.vector, ev_sc_user[sci % 2])
                    e = nc.vector.tensor_reduce(nm[:, sl:sl + 1], sro[:, 0:E],
                                                axis=AX.X, op=ALU.max, negate=True)
                    ev_nm = cx.dve.inc(e)
                    wait_ev(nc.scalar, ev_nm)
                    wait_ev(nc.scalar, ev_avdone[sl])
                    a = nc.scalar.activation(p_sb[:, sl * SEQ: sl * SEQ + E],
                                             sro[:, 0:E], AF.Exp,
                                             bias=nm[:, sl:sl + 1],
                                             accum_out=rsum[:, sl:sl + 1])
                    ev_exp[sl] = cx.act.inc(a)

                def emit_bcpath(c):
                    sl = c % 2
                    wait_ev(nc.vector, ev_exp[sl])
                    e = nc.vector.reciprocal(rsum[:, sl:sl + 1], rsum[:, sl:sl + 1])
                    ev_rc = cx.dve.inc(e)
                    wait_ev(nc.tensor, ev_rc)
                    wait_ev(nc.tensor, ev_pb[0])
                    trr = nc.tensor.transpose(pb[0:1, 0:128], rsum[:, sl:sl + 1],
                                              idf[:, :])
                    ev_trr = cx.pe.inc(trr)
                    wait_ev(nc.vector, ev_trr)
                    e = nc.vector.tensor_copy(rtr[:, sl * 128:(sl + 1) * 128],
                                              pb[0:1, 0:128])
                    ev_rtr = cx.dve.inc(e)
                    wait_ev(nc.tensor, ev_rtr)
                    if c == 0:
                        wait_ev(nc.tensor, mset)
                    bmm = nc.tensor.matmul(pb[:, :], ones1[:, :],
                                           rtr[:, sl * 128:(sl + 1) * 128],
                                           start=True, stop=True)
                    ev_bmm = cx.pe.inc(bmm)
                    wait_ev(nc.vector, ev_bmm)
                    e = nc.vector.tensor_copy(bc_sb[:, sl * 128:(sl + 1) * 128],
                                              pb[:, :])
                    ev_bc[sl] = cx.dve.inc(e)
                    ev_pb[0] = ev_bc[sl]

                def emit_av(c):
                    nonlocal tri
                    sl = c % 2
                    psl = p_sb[:, sl * SEQ:(sl + 1) * SEQ]
                    ev_mm_i = {}
                    for i in range(c + 2):
                        if i <= c:
                            ts_ = tri % 2
                            vs = tri % 4
                            wait_ev(nc.tensor, ev_exp[sl])
                            wait_ev(nc.tensor, ev_pt_free[ts_])
                            tr = nc.tensor.transpose(pt_ps[ts_][:, :],
                                                     psl[:, i * 128:(i + 1) * 128],
                                                     idb[:, :])
                            ev_tr = cx.pe.inc(tr)
                            wait_ev(nc.vector, ev_tr)
                            wait_ev(nc.vector, ev_pT_user[vs])
                            e = nc.vector.tensor_copy(pT_sb[:, vs * 128:(vs + 1) * 128],
                                                      pt_ps[ts_][:, :])
                            ev_pt_free[ts_] = cx.dve.inc(e)
                            ev_mm_i[i] = (vs, ev_pt_free[ts_])
                            tri += 1
                        j = i - 1
                        if j >= 0:
                            vs, ev_ready = ev_mm_i.pop(j)
                            wait_ev(nc.tensor, ev_ready)
                            if j == 0:
                                wait_ev(nc.tensor, ev_o[sl])
                            mm = nc.tensor.matmul(po[sl][:, :],
                                                  lv[:, j * 128:(j + 1) * 128],
                                                  pT_sb[:, vs * 128:(vs + 1) * 128],
                                                  start=(j == 0), stop=(j == c))
                            ev_av = cx.pe.inc(mm)
                            ev_pT_user[vs] = ev_av
                    ev_avdone[sl] = ev_av
                    wait_ev(nc.vector, ev_av)
                    wait_ev(nc.vector, ev_bc[sl])
                    s_sto[sl].wait(nc.vector)
                    e = nc.vector.tensor_tensor(o_bf[:, sl * 128:(sl + 1) * 128],
                                                po[sl][:, :],
                                                bc_sb[:, sl * 128:(sl + 1) * 128],
                                                ALU.mult)
                    ev_o[sl] = cx.dve.inc(e)
                    wait_ev(nc.sync, ev_o[sl])
                    cx.dma(nc.sync, o_hat[fr:fr + 128, bc + c * 128: bc + (c + 1) * 128],
                           o_bf[:, sl * 128:(sl + 1) * 128], s_sto[sl])

                for c in range(NCH + 1):
                    if c < NCH:
                        emit_scores_softmax(c)
                    if c >= 1:
                        emit_av(c - 1)
                    if c < NCH:
                        emit_bcpath(c)
                ev_qkv_user = ev_avdone[(NCH - 1) % 2]


def emit_decode(nc, cx, packed, scales, out_T, R, WP, tag="d"):
    CC = 512
    with ExitStack() as est:
        sb = lambda n, sh, dt: est.enter_context(nc.sbuf_tensor(tag + n, sh, dt))
        psb = lambda n, sh, dt=F32: est.enter_context(nc.psum_tensor(tag + n, sh, dt))
        pk = sb("pk", [128, 2 * CC], I32)
        sc = sb("sc", [128, 2 * (CC // 16)], F32)
        nib = sb("nib", [128, CC], I32)
        cf = sb("cf", [128, CC], F32)
        s8 = sb("s8", [128, CC], F32)
        cm = sb("cm", [128, CC], F32)
        r1 = sb("r1", [128, CC], F32)
        r2 = sb("r2", [128, CC], F32)
        mg = sb("mg", [128, CC], F32)
        sn = sb("sn", [128, CC], F32)
        dec = sb("dec", [128, 2 * 2 * CC], BF16)
        tev = sb("tev", [128, 4 * 128], BF16)
        idb = sb("id", [128, 128], BF16)
        ptp = [psb("t0", [128, 128], BF16), psb("t1", [128, 128], BF16)]

        s_pk = [cx.sem(tag + "_p0"), cx.sem(tag + "_p1")]
        s_st = [cx.sem(tag + "_st%d" % i) for i in range(4)]

        ident = nc.inline_tensor(np.eye(128, dtype=np.float32), name="ident_" + tag)
        cx.dma(nc.gpsimd, idb[:, :], ident.ap(), s_pk[0])
        s_id_n = s_pk[0].n

        NRT_ = R // 128
        NCC = WP // CC
        ev_user = [None, None]
        ev_tfree = [None, None]
        ev_dec_free = [None, None]
        ev_dec_user = None
        ci = 0
        tri = 0
        for rt in range(NRT_):
            for cc in range(NCC):
                slot = ci % 2
                wait_ev(nc.sync, ev_user[slot])
                cx.dma(nc.sync, pk[:, slot * CC:(slot + 1) * CC],
                       packed[rt * 128:(rt + 1) * 128, cc * CC:(cc + 1) * CC],
                       s_pk[slot])
                ev_ld = cx.dma(nc.sync, sc[:, slot * (CC // 16):(slot + 1) * (CC // 16)],
                               scales[rt * 128:(rt + 1) * 128,
                                      cc * (CC // 16):(cc + 1) * (CC // 16)], s_pk[slot])
                wait_ev(nc.vector, ev_ld)
                pksl = pk[:, slot * CC:(slot + 1) * CC]
                scsl = sc[:, slot * (CC // 16):(slot + 1) * (CC // 16)]
                dsel = dec[:, slot * 2 * CC:(slot + 1) * 2 * CC]
                for plane in range(2):
                    if plane == 0:
                        wait_ev(nc.vector, ev_dec_free[slot])
                        e = nc.vector.tensor_scalar(nib[:, :], pksl, 15, None,
                                                    ALU.bitwise_and)
                    else:
                        e = nc.vector.tensor_scalar(nib[:, :], pksl, 4, 15,
                                                    ALU.logical_shift_right,
                                                    ALU.bitwise_and)
                        ev_user[slot] = cx.dve.inc(e)
                    nc.vector.tensor_copy(cf[:, :], nib[:, :])
                    nc.vector.tensor_scalar(s8[:, :], cf[:, :], 8.0, None, ALU.is_ge)
                    nc.vector.tensor_scalar(cm[:, :], s8[:, :], 8.0, None, ALU.mult)
                    nc.vector.tensor_tensor(cm[:, :], cf[:, :], cm[:, :], ALU.subtract)
                    nc.vector.tensor_scalar(r1[:, :], cm[:, :], -4.0, 0.0,
                                            ALU.add, ALU.max)
                    nc.vector.tensor_scalar(r2[:, :], cm[:, :], -6.0, 0.0,
                                            ALU.add, ALU.max)
                    nc.vector.tensor_tensor(mg[:, :], cm[:, :], r1[:, :], ALU.add)
                    nc.vector.tensor_scalar(mg[:, :], mg[:, :], 0.5, None, ALU.mult)
                    nc.vector.tensor_tensor(mg[:, :], mg[:, :], r2[:, :], ALU.add)
                    nc.vector.tensor_scalar(sn[:, :], s8[:, :], -2.0, 1.0,
                                            ALU.mult, ALU.add)
                    nc.vector.tensor_tensor(sn[:, :], sn[:, :], mg[:, :], ALU.mult)
                    scb = scsl.unsqueeze(2).broadcast_to([128, CC // 16, 16])
                    dsl = dsel[:, plane:2 * CC:2].rearrange("p (b r) -> p b r", r=16)
                    ssl = sn[:, :].rearrange("p (b r) -> p b r", r=16)
                    e = nc.vector.tensor_tensor(dsl, ssl, scb, ALU.mult)
                    ev_dec_user = cx.dve.inc(e)
                ev_dec = ev_dec_user
                for i in range(2 * CC // 128):
                    ts_ = tri % 2
                    vs = tri % 4
                    wait_ev(nc.tensor, ev_dec)
                    wait_ev(nc.tensor, ev_tfree[ts_])
                    if rt == 0 and cc == 0 and i == 0:
                        nc.tensor.wait_ge(s_pk[0].h, s_id_n)
                    tr = nc.tensor.transpose(ptp[ts_][:, :],
                                             dsel[:, i * 128:(i + 1) * 128], idb[:, :])
                    ev_tr = cx.pe.inc(tr)
                    if i == 2 * CC // 128 - 1:
                        ev_dec_free[slot] = ev_tr
                    wait_ev(nc.scalar, ev_tr)
                    s_st[vs].wait(nc.scalar)
                    a = nc.scalar.activation(tev[:, vs * 128:(vs + 1) * 128],
                                             ptp[ts_][:, :], AF.Copy)
                    ev_tfree[ts_] = cx.act.inc(a)
                    wait_ev(nc.sync, ev_tfree[ts_])
                    r0 = cc * 2 * CC + i * 128
                    cx.dma(nc.sync, out_T[r0:r0 + 128, rt * 128:(rt + 1) * 128],
                           tev[:, vs * 128:(vs + 1) * 128], s_st[vs])
                    tri += 1
                ci += 1


def emit_l1ffn(nc, cx, x_bf, x_fm, wgT, wuT, wdT, out_tm, tag="f", ag_layout=False):
    TT = 512
    TP = 1024
    NP = T // TP
    NG = 16
    NT_ = 8
    C0 = 0.7978845608028654
    C1 = 0.044715
    with ExitStack() as est:
        sb = lambda n, sh, dt: est.enter_context(nc.sbuf_tensor(tag + n, sh, dt))
        psb = lambda n, sh, dt=F32: est.enter_context(nc.psum_tensor(tag + n, sh, dt))
        xs = sb("xs", [128, KC * TP], BF16)
        acc = sb("acc", [128, KC * TP], F32)
        wgu = sb("wgu", [128, 2 * 2 * KC * 128], BF16)
        wdd = sb("wdd", [128, 2 * NT_ * 128], BF16)
        ih = sb("ih", [128, NT_ * TP], BF16)
        sq2 = sb("sq2", [128, TT], F32)
        g3 = sb("g3", [128, TT], F32)
        tnh = sb("tnh", [128, TT], F32)
        gg = sb("gg", [128, TT], F32)
        xr_sb = sb("xr", [128, 2 * TT], F32)
        rs_sb = sb("rsb", [128, KC * TT], F32)
        sq_sb = sb("sq", [128, TT], F32)
        mrow = sb("mr", [1, TT], F32)
        m2row = sb("m2", [1, TT], F32)
        srow = sb("srw", [1, TT], F32)
        ones = sb("on", [128, 1], F32)
        ones1 = sb("o1", [1, 128], F32)
        y_sb = sb("y", [128, 2 * TT], F32)
        yt_sb = sb("yt", [128, 2 * 128], F32)
        bcm_sb = sb("bcm", [128, TT], F32)
        bcr_sb = sb("bcr", [128, TT], F32)
        idf = sb("idf", [128, 128], F32)
        pg = [psb("pg0", [128, TT]), psb("pg1", [128, TT])]
        pu = [psb("pu0", [128, TT]), psb("pu1", [128, TT])]
        pdn = [psb("pd0", [128, TT]), psb("pd1", [128, TT])]
        p_sum = psb("psm", [1, TT])
        p_sq = psb("psq", [1, TT])
        p_bm = pdn[0]
        p_br = pdn[1]
        gf = sb("gf", [128, 2 * TT], F32)

        s_x = cx.sem(tag + "_x")
        s_xr = [cx.sem(tag + "_xr0"), cx.sem(tag + "_xr1")]
        s_wgu = [cx.sem(tag + "_wg0"), cx.sem(tag + "_wg1")]
        s_wd = [cx.sem(tag + "_wd0"), cx.sem(tag + "_wd1")]
        s_y = [cx.sem(tag + "_y0"), cx.sem(tag + "_y1")]
        s_cld = cx.sem(tag + "_cld")

        identf = nc.inline_tensor(np.eye(128, dtype=np.float32), name="ident_" + tag)
        cx.dma(nc.sync, idf[:, :], identf.ap(), s_cld)
        nc.vector.memset(ones[:, :], 1.0)
        mset = cx.dve.inc(nc.vector.memset(ones1[:, :], 1.0))

        ev_x_user = None
        ev_wgu_user = [None, None]
        ev_wdd_user = [None, None]
        ev_pd_free = [None, None]
        ev_pg_free = [None, None]
        ev_pu_free = [None, None]
        ev_acc_user = None
        ev_ih_user = None
        ev_gf_user = [None, None]
        ev_pbm_user = None
        ev_statsread = None
        ev_m2_prev = None
        gi = 0
        di = 0
        for p in range(NP):
            pc = p * TP
            wait_ev(nc.sync, ev_x_user)
            for c in range(KC):
                cx.dma(nc.sync, xs[:, c * TP:(c + 1) * TP],
                       x_bf[c * 128:(c + 1) * 128, pc:pc + TP], s_x)
            for ng in range(NG):
                n0 = ng * 1024
                for nt in range(NT_):
                    wslot = gi % 2
                    wait_ev(nc.gpsimd, ev_wgu_user[wslot])
                    base = wslot * 2 * KC * 128
                    n_abs = n0 + nt * 128
                    if ag_layout:
                        blk = n_abs // 2048
                        ncol = n_abs % 2048
                        wg_sl = wgT[blk * 2048:(blk + 1) * 2048, ncol:ncol + 128]
                        wu_sl = wuT[blk * 2048:(blk + 1) * 2048, ncol:ncol + 128]
                    else:
                        wg_sl = wgT[:, n_abs:n_abs + 128]
                        wu_sl = wuT[:, n_abs:n_abs + 128]
                    cx.dma(nc.gpsimd,
                           wgu[:, base: base + KC * 128].rearrange(
                               "p (g h) -> p g h", g=KC),
                           wg_sl.rearrange("(g p) h -> p g h", p=128), s_wgu[wslot])
                    ev_w2 = cx.dma(nc.gpsimd,
                                   wgu[:, base + KC * 128: base + 2 * KC * 128].rearrange(
                                       "p (g h) -> p g h", g=KC),
                                   wu_sl.rearrange("(g p) h -> p g h", p=128), s_wgu[wslot])
                    for th in range(TP // TT):
                        tc = th * TT
                        ps2 = th
                        gfs = gf[:, th * TT:(th + 1) * TT]
                        nc.tensor.wait_ge(s_x.h, s_x.n)
                        wait_ev(nc.tensor, ev_w2)
                        wait_ev(nc.tensor, ev_pg_free[ps2])
                        for c in range(KC):
                            mmg = nc.tensor.matmul(
                                pg[ps2][:, :], wgu[:, base + c * 128: base + (c + 1) * 128],
                                xs[:, c * TP + tc: c * TP + tc + TT],
                                start=(c == 0), stop=(c == KC - 1))
                        ev_g = cx.pe.inc(mmg)
                        wait_ev(nc.tensor, ev_pu_free[ps2])
                        for c in range(KC):
                            mmu = nc.tensor.matmul(
                                pu[ps2][:, :],
                                wgu[:, base + (KC + c) * 128: base + (KC + c + 1) * 128],
                                xs[:, c * TP + tc: c * TP + tc + TT],
                                start=(c == 0), stop=(c == KC - 1))
                        ev_u = cx.pe.inc(mmu)
                        if th == TP // TT - 1:
                            ev_wgu_user[wslot] = ev_u
                        wait_ev(nc.scalar, ev_g)
                        wait_ev(nc.scalar, ev_gf_user[ps2])
                        a = nc.scalar.activation(gfs, pg[ps2][:, :], AF.Copy)
                        ev_pg_free[ps2] = cx.act.inc(a)
                        wait_ev(nc.vector, ev_pg_free[ps2])
                        e = nc.vector.tensor_tensor(sq2[:, :], gfs, gfs, ALU.mult)
                        nc.vector.tensor_tensor(g3[:, :], sq2[:, :], gfs, ALU.mult)
                        nc.vector.tensor_scalar(g3[:, :], g3[:, :], C1, None, ALU.mult)
                        e = nc.vector.tensor_tensor(g3[:, :], g3[:, :], gfs, ALU.add)
                        ev_a = cx.dve.inc(e)
                        wait_ev(nc.scalar, ev_a)
                        a = nc.scalar.activation(tnh[:, :], g3[:, :], AF.Tanh, scale=C0)
                        ev_t = cx.act.inc(a)
                        wait_ev(nc.vector, ev_t)
                        nc.vector.tensor_scalar(gg[:, :], tnh[:, :], 1.0, 0.5,
                                                ALU.add, ALU.mult)
                        e = nc.vector.tensor_tensor(gg[:, :], gg[:, :], gfs, ALU.mult)
                        ev_gg = cx.dve.inc(e)
                        ev_gf_user[ps2] = ev_gg
                        wait_ev(nc.vector, ev_gg)
                        wait_ev(nc.vector, ev_u)
                        if nt == 0 and th == 0:
                            wait_ev(nc.vector, ev_ih_user)
                        dst = ih[:, nt * TP + tc: nt * TP + tc + TT]
                        e = nc.vector.tensor_tensor(dst, gg[:, :], pu[ps2][:, :], ALU.mult)
                        ev_ih = cx.dve.inc(e)
                        ev_pu_free[ps2] = ev_ih
                    gi += 1
                for dm in range(KC):
                    wslot = di % 2
                    wait_ev(nc.gpsimd, ev_wdd_user[wslot])
                    if ag_layout:
                        dblk = dm // 2
                        dcol = (dm % 2) * 128
                        wd_sl = wdT[dblk * TG + n0: dblk * TG + n0 + 1024,
                                    dcol:dcol + 128]
                    else:
                        wd_sl = wdT[n0:n0 + 1024, dm * 128:(dm + 1) * 128]
                    ev_wd = cx.dma(nc.gpsimd,
                                   wdd[:, wslot * NT_ * 128:(wslot + 1) * NT_ * 128].rearrange(
                                       "p (g d) -> p g d", g=NT_),
                                   wd_sl.rearrange("(g p) d -> p g d", p=128), s_wd[wslot])
                    for th in range(TP // TT):
                        tc = th * TT
                        ps = th
                        wait_ev(nc.tensor, ev_wd)
                        wait_ev(nc.tensor, ev_ih)
                        wait_ev(nc.tensor, ev_pd_free[ps])
                        for nt in range(NT_):
                            mmd = nc.tensor.matmul(
                                pdn[ps][:, :],
                                wdd[:, wslot * NT_ * 128 + nt * 128:
                                    wslot * NT_ * 128 + (nt + 1) * 128],
                                ih[:, nt * TP + tc: nt * TP + tc + TT],
                                start=(nt == 0), stop=(nt == NT_ - 1))
                        ev_d = cx.pe.inc(mmd)
                        if th == TP // TT - 1:
                            ev_wdd_user[wslot] = ev_d
                        wait_ev(nc.vector, ev_d)
                        wait_ev(nc.vector, ev_acc_user)
                        asl = acc[:, dm * TP + tc: dm * TP + tc + TT]
                        if ng == 0:
                            ev_pd_free[ps] = cx.dve.inc(
                                nc.vector.tensor_copy(asl, pdn[ps][:, :]))
                        else:
                            ev_pd_free[ps] = cx.dve.inc(
                                nc.vector.tensor_tensor(asl, pdn[ps][:, :], asl, ALU.add))
                    di += 1
                ev_ih_user = ev_d
            ev_x_user = ev_d

            ev_xr2_user = [None, None]
            for th in range(TP // TT):
                tc = th * TT
                for dm in range(KC):
                    xslot = dm % 2
                    wait_ev(nc.gpsimd, ev_xr2_user[xslot])
                    ev_xl = cx.dma(nc.gpsimd, xr_sb[:, xslot * TT:(xslot + 1) * TT],
                                   x_fm[dm * 128:(dm + 1) * 128, pc + tc: pc + tc + TT],
                                   s_xr[xslot])
                    wait_ev(nc.vector, ev_xl)
                    if th == 0 and dm == 0:
                        wait_ev(nc.vector, ev_pd_free[0])
                        wait_ev(nc.vector, ev_pd_free[1])
                    dst = rs_sb[:, dm * TT:(dm + 1) * TT]
                    e2 = nc.vector.tensor_tensor(dst, acc[:, dm * TP + tc: dm * TP + tc + TT],
                                                 xr_sb[:, xslot * TT:(xslot + 1) * TT],
                                                 ALU.add)
                    ev_rs = cx.dve.inc(e2)
                    ev_xr2_user[xslot] = ev_rs
                    wait_ev(nc.scalar, ev_rs)
                    wait_ev(nc.scalar, ev_m2_prev)
                    a = nc.scalar.activation(sq_sb[:, :], dst, AF.Square)
                    ev_sq = cx.act.inc(a)
                    wait_ev(nc.tensor, ev_rs)
                    if dm == 0:
                        wait_ev(nc.tensor, mset)
                        wait_ev(nc.tensor, ev_statsread)
                    nc.tensor.matmul(p_sum[:, :], ones[:, :], dst,
                                     start=(dm == 0), stop=(dm == KC - 1))
                    wait_ev(nc.tensor, ev_sq)
                    m2 = nc.tensor.matmul(p_sq[:, :], ones[:, :], sq_sb[:, :],
                                          start=(dm == 0), stop=(dm == KC - 1))
                    ev_m2 = cx.pe.inc(m2)
                    ev_m2_prev = ev_m2
                wait_ev(nc.scalar, ev_m2)
                a = nc.scalar.activation(mrow[:, :], p_sum[:, :], AF.Copy, scale=1.0 / D)
                cx.act.inc(a)
                a = nc.scalar.activation(m2row[:, :], p_sq[:, :], AF.Copy, scale=1.0 / D)
                ev_cp = cx.act.inc(a)
                ev_statsread = ev_cp
                wait_ev(nc.vector, ev_cp)
                nc.vector.tensor_tensor(srow[:, :], mrow[:, :], mrow[:, :], ALU.mult)
                e2 = nc.vector.tensor_tensor(srow[:, :], m2row[:, :], srow[:, :],
                                             ALU.subtract)
                ev_var = cx.dve.inc(e2)
                wait_ev(nc.scalar, ev_var)
                a = nc.scalar.activation(srow[:, :], srow[:, :], AF.Sqrt, bias=EPS_LN)
                ev_std = cx.act.inc(a)
                wait_ev(nc.vector, ev_std)
                e2 = nc.vector.reciprocal(srow[:, :], srow[:, :])
                ev_rstd = cx.dve.inc(e2)
                wait_ev(nc.tensor, ev_rstd)
                wait_ev(nc.tensor, ev_pbm_user)
                wait_ev(nc.tensor, ev_pd_free[0])
                wait_ev(nc.tensor, ev_pd_free[1])
                bm = nc.tensor.matmul(p_bm[:, 0:TT], ones1[:, :], mrow[:, :],
                                      start=True, stop=True)
                cx.pe.inc(bm)
                br = nc.tensor.matmul(p_br[:, 0:TT], ones1[:, :], srow[:, :],
                                      start=True, stop=True)
                ev_bc = cx.pe.inc(br)
                wait_ev(nc.vector, ev_bc)
                nc.vector.tensor_copy(bcm_sb[:, :], p_bm[:, 0:TT])
                e2 = nc.vector.tensor_copy(bcr_sb[:, :], p_br[:, 0:TT])
                ev_bcc = cx.dve.inc(e2)
                ev_pd_free[0] = ev_bcc
                ev_pd_free[1] = ev_bcc
                for dm in range(KC):
                    yslot = dm % 2
                    wait_ev(nc.vector, ev_bcc)
                    s_y[yslot].wait(nc.vector)
                    ysl = y_sb[:, yslot * TT:(yslot + 1) * TT]
                    nc.vector.tensor_tensor(ysl, rs_sb[:, dm * TT:(dm + 1) * TT],
                                            bcm_sb[:, :], ALU.subtract)
                    e2 = nc.vector.tensor_tensor(ysl, ysl, bcr_sb[:, :], ALU.mult)
                    ev_y = cx.dve.inc(e2)
                    if dm == KC - 1:
                        ev_pbm_user = ev_y
                    for i in range(4):
                        tslot = dm % 2
                        wait_ev(nc.tensor, ev_y)
                        wait_ev(nc.tensor, ev_pd_free[tslot])
                        s_cld.wait(nc.tensor)
                        tr = nc.tensor.transpose(pdn[tslot][:, 0:128],
                                                 ysl[:, i * 128:(i + 1) * 128],
                                                 idf[:, :])
                        ev_tr = cx.pe.inc(tr)
                        wait_ev(nc.vector, ev_tr)
                        s_y[yslot].wait(nc.vector)
                        e2 = nc.vector.tensor_copy(yt_sb[:, yslot * 128:(yslot + 1) * 128],
                                                   pdn[tslot][:, 0:128])
                        ev_yt = cx.dve.inc(e2)
                        ev_pd_free[tslot] = ev_yt
                        wait_ev(nc.sync, ev_yt)
                        r0 = pc + tc + i * 128
                        cx.dma(nc.sync, out_tm[r0:r0 + 128, dm * 128:(dm + 1) * 128],
                               yt_sb[:, yslot * 128:(yslot + 1) * 128], s_y[yslot])
                ev_acc_user = ev_y


from concourse.bass_utils import run_bass_kernel_spmd

ALL_CONSTS = [EPS_LN, EPS_RMS, 1.0 / D, 1.0 / 128.0, -4.0, -6.0,
              0.7978845608028654]


def _run(nc, in_maps):
    res = run_bass_kernel_spmd(nc, in_maps, core_ids=list(range(NCORES)))
    return res.results


def _bass():
    return bass.Bass("TRN2", target_bir_lowering=False)


def launch_xprep(xfm_list):
    nc = _bass()
    cx = Ctx(nc)
    xf = nc.dram_tensor("x_fm", [D, T], F32, kind="ExternalInput").ap()
    xb = nc.dram_tensor("x_bf", [D, T], BF16, kind="ExternalOutput").ap()
    with nc.sbuf_tensor("xp_t", [128, T], BF16) as tb:
        s_ld = cx.sem("xp_ld")
        s_st = cx.sem("xp_st")
        for c in range(KC):
            s_st.wait(nc.gpsimd)
            ev = cx.dma(nc.gpsimd, tb[:, :], xf[c * 128:(c + 1) * 128, :], s_ld)
            wait_ev(nc.sync, ev)
            cx.dma(nc.sync, xb[c * 128:(c + 1) * 128, :], tb[:, :], s_st)
        cx.finish()
    return _run(nc, [{"x_fm": x} for x in xfm_list])


def launch_qkvz_l0(xg, wqkvz_list, cos, sin):
    nc = _bass()
    cx = Ctx(nc)
    cx.setup_consts(ALL_CONSTS)
    xg_t = nc.dram_tensor("xg", [8 * D, T], BF16, kind="ExternalInput").ap()
    w_t = nc.dram_tensor("wqkvz", [D, 896], F32, kind="ExternalInput").ap()
    cos_t = nc.dram_tensor("cos_t", [64, SEQ], F32, kind="ExternalInput").ap()
    sin_t = nc.dram_tensor("sin_t", [64, SEQ], F32, kind="ExternalInput").ap()
    q_t = nc.dram_tensor("q_fm", [256, TG], BF16, kind="ExternalOutput").ap()
    k_t = nc.dram_tensor("k_fm", [256, TG], BF16, kind="ExternalOutput").ap()
    v_t = nc.dram_tensor("v_tm", [TG, 256], BF16, kind="ExternalOutput").ap()
    z_t = nc.dram_tensor("z_fm", [128, TG], BF16, kind="ExternalOutput").ap()
    emit_qkvz_l0(nc, cx, xg_t, w_t, q_t, k_t, v_t, z_t, cos_t, sin_t)
    cx.finish()
    return _run(nc, [{"xg": xg, "wqkvz": w, "cos_t": cos, "sin_t": sin}
                     for w in wqkvz_list])


def launch_lightning(qkvz_res, consts_list, wg2_list, nw_list):
    nc = _bass()
    cx = Ctx(nc)
    cx.setup_consts(ALL_CONSTS)
    qt = nc.dram_tensor("q_fm", [256, TG], BF16, kind="ExternalInput").ap()
    kt = nc.dram_tensor("k_fm", [256, TG], BF16, kind="ExternalInput").ap()
    vt = nc.dram_tensor("v_tm", [TG, 256], BF16, kind="ExternalInput").ap()
    zt = nc.dram_tensor("z_fm", [128, TG], BF16, kind="ExternalInput").ap()
    dtt = nc.dram_tensor("dt_c", [256, 128], F32, kind="ExternalInput").ap()
    dqt = nc.dram_tensor("dq_c", [256, 128], F32, kind="ExternalInput").ap()
    dkt = nc.dram_tensor("dk_c", [256, 1], F32, kind="ExternalInput").ap()
    lct = nc.dram_tensor("lamc_c", [256, 1], F32, kind="ExternalInput").ap()
    wgt = nc.dram_tensor("wg2_c", [128, 256], F32, kind="ExternalInput").ap()
    nwt = nc.dram_tensor("nw_c", [256, 1], F32, kind="ExternalInput").ap()
    ot = nc.dram_tensor("o_hat", [256, TG], BF16, kind="ExternalOutput").ap()
    emit_lightning(nc, cx, qt, kt, vt, zt, ot, dtt, dqt, dkt, lct, wgt, nwt)
    cx.finish()
    in_maps = []
    for c in range(NCORES):
        dt_c, dq_c, dk_c, lamc_c = consts_list[c]
        in_maps.append({
            "q_fm": qkvz_res[c]["q_fm"], "k_fm": qkvz_res[c]["k_fm"],
            "v_tm": qkvz_res[c]["v_tm"], "z_fm": qkvz_res[c]["z_fm"],
            "dt_c": dt_c.reshape(256, 128), "dq_c": dq_c.reshape(256, 128),
            "dk_c": dk_c.reshape(256, 1),
            "lamc_c": np.repeat(lamc_c, 128).reshape(256, 1),
            "wg2_c": wg2_list[c], "nw_c": nw_list[c],
        })
    return _run(nc, in_maps)


def launch_woln(og_my_list, wo, xfm_list, bias=None):
    nc = _bass()
    cx = Ctx(nc)
    cx.setup_consts(ALL_CONSTS)
    og_t = nc.dram_tensor("og", [D, T], BF16, kind="ExternalInput").ap()
    wo_t = nc.dram_tensor("wo", [D, D], F32, kind="ExternalInput").ap()
    xr_t = nc.dram_tensor("xres", [D, T], F32, kind="ExternalInput").ap()
    b_t = (nc.dram_tensor("bias", [D, 1], F32, kind="ExternalInput").ap()
           if bias is not None else None)
    y_t = nc.dram_tensor("y_fm", [D, T], F32, kind="ExternalOutput").ap()
    yb_t = nc.dram_tensor("y_bf", [D, T], BF16, kind="ExternalOutput").ap()
    emit_woln(nc, cx, og_t, wo_t, xr_t, y_t, yb_t, bias=b_t, xcol0=0)
    cx.finish()
    in_maps = []
    for c in range(NCORES):
        m = {"og": og_my_list[c], "wo": wo, "xres": xfm_list[c]}
        if bias is not None:
            m["bias"] = bias
        in_maps.append(m)
    return _run(nc, in_maps)


def launch_moe(prev, gate_w, wg, wu, wd, swg, swu, swd):
    nc = _bass()
    cx = Ctx(nc)
    cx.setup_consts(ALL_CONSTS)
    xb_t = nc.dram_tensor("x_bf", [D, T], BF16, kind="ExternalInput").ap()
    xf_t = nc.dram_tensor("x_fm", [D, T], F32, kind="ExternalInput").ap()
    gw_t = nc.dram_tensor("gate_w", [D, 8], F32, kind="ExternalInput").ap()
    wg_t = nc.dram_tensor("wg", [8, D, 1408], F32, kind="ExternalInput").ap()
    wu_t = nc.dram_tensor("wu", [8, D, 1408], F32, kind="ExternalInput").ap()
    wd_t = nc.dram_tensor("wd", [8, 1408, D], F32, kind="ExternalInput").ap()
    swg_t = nc.dram_tensor("swg", [D, 2816], F32, kind="ExternalInput").ap()
    swu_t = nc.dram_tensor("swu", [D, 2816], F32, kind="ExternalInput").ap()
    swd_t = nc.dram_tensor("swd", [2816, D], F32, kind="ExternalInput").ap()
    y_t = nc.dram_tensor("y_fm", [D, T], F32, kind="ExternalOutput").ap()
    yb_t = nc.dram_tensor("y_bf", [D, T], BF16, kind="ExternalOutput").ap()
    emit_moe(nc, cx, xb_t, xf_t, gw_t, wg_t, wu_t, wd_t, swg_t, swu_t, swd_t,
             y_t, yb_t)
    cx.finish()
    return _run(nc, [{"x_bf": prev[c]["y_bf"], "x_fm": prev[c]["y_fm"],
                      "gate_w": gate_w, "wg": wg, "wu": wu, "wd": wd,
                      "swg": swg, "swu": swu, "swd": swd}
                     for c in range(NCORES)])


def launch_qkv_l1(xg, wqkv_list, bqkv_list):
    nc = _bass()
    cx = Ctx(nc)
    cx.setup_consts(ALL_CONSTS)
    xg_t = nc.dram_tensor("xg", [8 * D, T], BF16, kind="ExternalInput").ap()
    w_t = nc.dram_tensor("wqkv", [D, 768], F32, kind="ExternalInput").ap()
    b_t = nc.dram_tensor("bqkv", [768, 1], F32, kind="ExternalInput").ap()
    q_t = nc.dram_tensor("q_fm", [256, TG], BF16, kind="ExternalOutput").ap()
    k_t = nc.dram_tensor("k_fm", [256, TG], BF16, kind="ExternalOutput").ap()
    v_t = nc.dram_tensor("v_tm", [TG, 256], BF16, kind="ExternalOutput").ap()
    emit_qkv_l1(nc, cx, xg_t, w_t, b_t, q_t, k_t, v_t)
    cx.finish()
    return _run(nc, [{"xg": xg, "wqkv": wqkv_list[c], "bqkv": bqkv_list[c]}
                     for c in range(NCORES)])


def launch_causal(qkv_res, cmask):
    nc = _bass()
    cx = Ctx(nc)
    cx.setup_consts(ALL_CONSTS)
    qt = nc.dram_tensor("q_fm", [256, TG], BF16, kind="ExternalInput").ap()
    kt = nc.dram_tensor("k_fm", [256, TG], BF16, kind="ExternalInput").ap()
    vt = nc.dram_tensor("v_tm", [TG, 256], BF16, kind="ExternalInput").ap()
    cm_t = nc.dram_tensor("cmask", [512, 512], F32, kind="ExternalInput").ap()
    o_t = nc.dram_tensor("o_hat", [256, TG], BF16, kind="ExternalOutput").ap()
    emit_causal(nc, cx, qt, kt, vt, o_t, cm_t)
    cx.finish()
    return _run(nc, [{"q_fm": qkv_res[c]["q_fm"], "k_fm": qkv_res[c]["k_fm"],
                      "v_tm": qkv_res[c]["v_tm"], "cmask": cmask}
                     for c in range(NCORES)])


def launch_decode(gp_l, gs_l, up_l, us_l, dp_l, ds_l):
    nc = _bass()
    cx = Ctx(nc)
    cx.setup_consts(ALL_CONSTS)
    gp_t = nc.dram_tensor("gp", [2048, 1024], I32, kind="ExternalInput").ap()
    gs_t = nc.dram_tensor("gs", [2048, 64], F32, kind="ExternalInput").ap()
    up_t = nc.dram_tensor("up", [2048, 1024], I32, kind="ExternalInput").ap()
    us_t = nc.dram_tensor("us", [2048, 64], F32, kind="ExternalInput").ap()
    dp_t = nc.dram_tensor("dp", [256, 8192], I32, kind="ExternalInput").ap()
    ds_t = nc.dram_tensor("dsc", [256, 512], F32, kind="ExternalInput").ap()
    gT_t = nc.dram_tensor("gT", [2048, 2048], BF16, kind="ExternalOutput").ap()
    uT_t = nc.dram_tensor("uT", [2048, 2048], BF16, kind="ExternalOutput").ap()
    dT_t = nc.dram_tensor("dT", [16384, 256], BF16, kind="ExternalOutput").ap()
    emit_decode(nc, cx, gp_t, gs_t, gT_t, 2048, 1024, tag="dg")
    emit_decode(nc, cx, up_t, us_t, uT_t, 2048, 1024, tag="du")
    emit_decode(nc, cx, dp_t, ds_t, dT_t, 256, 8192, tag="dd")
    cx.finish()
    return _run(nc, [{"gp": gp_l[c], "gs": gs_l[c], "up": up_l[c], "us": us_l[c],
                      "dp": dp_l[c], "dsc": ds_l[c]} for c in range(NCORES)])


def launch_l1ffn(prev, wgT, wuT, wdT):
    nc = _bass()
    cx = Ctx(nc)
    cx.setup_consts(ALL_CONSTS)
    xb_t = nc.dram_tensor("xb", [D, T], BF16, kind="ExternalInput").ap()
    xf_t = nc.dram_tensor("xf", [D, T], F32, kind="ExternalInput").ap()
    wg_t = nc.dram_tensor("wgT", [D, TG], BF16, kind="ExternalInput").ap()
    wu_t = nc.dram_tensor("wuT", [D, TG], BF16, kind="ExternalInput").ap()
    wd_t = nc.dram_tensor("wdT", [TG, D], BF16, kind="ExternalInput").ap()
    o_t = nc.dram_tensor("o", [T, D], F32, kind="ExternalOutput").ap()
    emit_l1ffn(nc, cx, xb_t, xf_t, wg_t, wu_t, wd_t, o_t)
    cx.finish()
    return _run(nc, [{"xb": prev[c]["y_bf"], "xf": prev[c]["y_fm"],
                      "wgT": wgT, "wuT": wuT, "wdT": wdT}
                     for c in range(NCORES)])


def kernel(**inputs):
    g = {k: np.asarray(v) for k, v in inputs.items()}
    x = g["L0_attn_x"].astype(np.float32, copy=False).reshape(TG, D)
    xfm = [np.ascontiguousarray(x[c * T:(c + 1) * T].T) for c in range(NCORES)]
    cos, sin = lrpe_tables()
    cmask = causal_masks().reshape(512, 512)

    r0 = launch_xprep(xfm)
    xg = np.concatenate([r0[c]["x_bf"] for c in range(NCORES)], axis=0)
    wqkvz = []
    wg2_l, nw_l, lc_l = [], [], []
    for c in range(NCORES):
        hsl = slice(2 * c * 128, (2 * c + 2) * 128)
        wqkvz.append(np.concatenate(
            [g["L0_attn_Wq"][:, hsl], g["L0_attn_Wk"][:, hsl],
             g["L0_attn_Wv"][:, hsl], g["L0_attn_Wg1"]], axis=1))
        wg2_l.append(np.ascontiguousarray(g["L0_attn_Wg2"][:, hsl]))
        nw_l.append(np.ascontiguousarray(g["L0_attn_norm_weight"][hsl]).reshape(256, 1))
        lc_l.append(lightning_consts(c))
    r1 = launch_qkvz_l0(xg, wqkvz, cos, sin)
    r2 = launch_lightning(r1, lc_l, wg2_l, nw_l)
    og = np.concatenate([r2[c]["o_hat"] for c in range(NCORES)], axis=0)
    og_my = [np.ascontiguousarray(og[:, c * T:(c + 1) * T]) for c in range(NCORES)]
    r3 = launch_woln(og_my, g["L0_attn_Wo"], xfm, bias=None)

    r4 = launch_moe(r3, g["L0_ffn_gate_w"], g["L0_ffn_w_gate"], g["L0_ffn_w_up"],
                    g["L0_ffn_w_down"], g["L0_ffn_sw_gate"], g["L0_ffn_sw_up"],
                    g["L0_ffn_sw_down"])

    x2g = np.concatenate([r4[c]["y_bf"] for c in range(NCORES)], axis=0)
    wqkv_l, bqkv_l = [], []
    for c in range(NCORES):
        h0, h1 = 2 * c, 2 * c + 1
        wqkv_l.append(np.concatenate(
            [g["L1_attn_W_Q"][h0], g["L1_attn_W_Q"][h1],
             g["L1_attn_W_K"][h0], g["L1_attn_W_K"][h1],
             g["L1_attn_W_V"][h0], g["L1_attn_W_V"][h1]], axis=1))
        bqkv_l.append(np.concatenate(
            [g["L1_attn_b_Q"][h0], g["L1_attn_b_Q"][h1],
             g["L1_attn_b_K"][h0], g["L1_attn_b_K"][h1],
             g["L1_attn_b_V"][h0], g["L1_attn_b_V"][h1]]).reshape(768, 1))
    r5 = launch_qkv_l1(x2g, wqkv_l, bqkv_l)
    r6 = launch_causal(r5, cmask)
    og2 = np.concatenate([r6[c]["o_hat"] for c in range(NCORES)], axis=0)
    og2_my = [np.ascontiguousarray(og2[:, c * T:(c + 1) * T]) for c in range(NCORES)]
    x2fm = [r4[c]["y_fm"] for c in range(NCORES)]
    wo1 = np.ascontiguousarray(g["L1_attn_W_O"].reshape(D, D))
    r7 = launch_woln(og2_my, wo1, x2fm, bias=g["L1_attn_b_O"].reshape(D, 1))

    gp_l = [np.ascontiguousarray(g["L1_ffn_gate_W_packed"][c * 2048:(c + 1) * 2048])
            for c in range(NCORES)]
    gs_l = [np.ascontiguousarray(g["L1_ffn_gate_scales"][c * 2048:(c + 1) * 2048])
            for c in range(NCORES)]
    up_l = [np.ascontiguousarray(g["L1_ffn_up_W_packed"][c * 2048:(c + 1) * 2048])
            for c in range(NCORES)]
    us_l = [np.ascontiguousarray(g["L1_ffn_up_scales"][c * 2048:(c + 1) * 2048])
            for c in range(NCORES)]
    dp_l = [np.ascontiguousarray(g["L1_ffn_down_W_packed"][c * 256:(c + 1) * 256])
            for c in range(NCORES)]
    ds_l = [np.ascontiguousarray(g["L1_ffn_down_scales"][c * 256:(c + 1) * 256])
            for c in range(NCORES)]
    r8 = launch_decode(gp_l, gs_l, up_l, us_l, dp_l, ds_l)
    wgT = np.concatenate([r8[c]["gT"] for c in range(NCORES)], axis=1)
    wuT = np.concatenate([r8[c]["uT"] for c in range(NCORES)], axis=1)
    wdT = np.concatenate([r8[c]["dT"] for c in range(NCORES)], axis=1)
    r9 = launch_l1ffn(r7, wgT, wuT, wdT)

    out = np.concatenate([r9[c]["o"] for c in range(NCORES)], axis=0)
    return out.reshape(NB, SEQ, D)
